# version 23 (final)
# speedup vs baseline: 1.1787x; 1.0010x over previous
import contextlib
import numpy as np
import ml_dtypes
import concourse.bass as bass
import concourse.mybir as mybir
from concourse.bass_utils import run_bass_kernel_spmd
from concourse.alu_op_type import AluOpType as alu

F32 = mybir.dt.float32
BF16 = mybir.dt.bfloat16
AF = mybir.ActivationFunctionType
AX = mybir.AxisListType.X
BF = ml_dtypes.bfloat16

S = 4096
D = 1024
NT = S // 128
NB = S // 512
NCH = D // 128
NEXP_G = 17


class Eng:
    def __init__(self, name, step):
        self.name = name
        self.step = step
        self.ops = []
        self.cnt = 0
        self.sem = None

    def op(self, f, inc=False):
        if self.step == 16 and self.cnt > 0:
            self.wait(self, self.cnt)
        if inc:
            self.cnt += self.step
            def g(e, f=f, self_=self):
                f(e).then_inc(self_.sem, self_.step)
            self.ops.append(g)
        else:
            self.ops.append(lambda e, f=f: f(e))
        return self.cnt

    def wait(self, other, val):
        if val and val > 0:
            self.ops.append(lambda e, o=other, v=val: e.wait_ge(o.sem, v))

    def replay(self, e):
        ops, self.ops = self.ops, []
        for f in ops:
            f(e)


class Ctx:
    def __init__(self, nc, stack):
        self.nc = nc
        self.stack = stack
        self.T = Eng("tensor", 1)
        self.V = Eng("vector", 1)
        self.S = Eng("scalar", 1)
        self.G = Eng("gpsimd", 16)
        self.Q = Eng("sync", 16)
        self.engines = [self.T, self.V, self.S, self.G, self.Q]
        for e in self.engines:
            e.sem = stack.enter_context(nc.semaphore("sem_" + e.name))
        self.uid = 0

    def name(self, base):
        self.uid += 1
        return f"{base}_{self.uid}"

    def sbuf(self, base, shape, dt, st=None, side=None):
        kw = {"side": side} if side else {}
        return (st or self.stack).enter_context(
            self.nc.sbuf_tensor(self.name(base), shape, dt, **kw))

    def psum(self, base, shape, dt, st=None):
        return (st or self.stack).enter_context(
            self.nc.psum_tensor(self.name(base), shape, dt))

    def dram(self, base, shape, dt):
        return self.nc.dram_tensor(self.name(base), shape, dt, kind="Internal").ap()

    def scope(self):
        return self.stack.enter_context(contextlib.ExitStack())

    def flush(self, name=None):
        for e in (self.G, self.Q):
            if e.ops:
                e.ops.append(lambda eng, e=e, v=e.cnt: eng.wait_ge(e.sem, v))
        block = self.nc.Block(self.name(name or "blk"))
        b = block.__enter__()
        for e in self.engines:
            if e.ops:
                getattr(b, e.name)(lambda eng, e=e: e.replay(eng))
        block.__exit__(None, None, None)
        for sc_ in getattr(self, "_close_after_flush", []):
            sc_.close()
        self._close_after_flush = []


def ap2(t_ap, off, rows, rstride, cols):
    return bass.AP(t_ap.tensor, t_ap.offset + off, [[rstride, rows], [1, cols]])


def ap3(t_ap, off, d0, s0, d1, s1, d2):
    return bass.AP(t_ap.tensor, t_ap.offset + off, [[s0, d0], [s1, d1], [1, d2]])



def ln_tile(ctx, x_sb, u_sb, b12, mv_cols, rstd_col, eps_col):
    V, Sc = ctx.V, ctx.S
    V.op(lambda v: v.bn_stats(b12[:, 0:6], x_sb[:, 0:512]))
    t_bn = V.op(lambda v: v.bn_stats(b12[:, 6:12], x_sb[:, 512:1024]), inc=True)
    V.wait(V, t_bn)
    t_mv = V.op(lambda v: v.bn_aggr(mv_cols, b12[:, :]), inc=True)
    Sc.wait(V, t_mv)
    t_ln = Sc.op(lambda s: s.activation(rstd_col, mv_cols[:, 1:2], AF.Ln,
                                        bias=eps_col[:, 0:1], scale=1.0), inc=True)
    Sc.wait(Sc, t_ln)
    t_rstd = Sc.op(lambda s: s.activation(rstd_col, rstd_col, AF.Exp, scale=-0.5),
                   inc=True)
    V.wait(Sc, t_rstd)
    V.wait(V, t_mv)
    t_u = V.op(lambda v: v.tensor_scalar(u_sb[:, :], x_sb[:, :], mv_cols[:, 0:1],
                                         rstd_col, op0=alu.subtract, op1=alu.mult),
               inc=True)
    return t_u



def emit_weightprep(ctx, W, need, flush=False):
    out = {}
    G, Q, V = ctx.G, ctx.Q, ctx.V
    tok = 0
    if "WrT" in need:
        out["WrT"] = ctx.dram("WrT_bf", [16, 1024, 1024], BF16)
        for e in range(16):
            tok = G.op(lambda g, e=e: g.dma_start(
                ap2(out["WrT"], e * 1024 * 1024, 1024, 1024, 1024),
                ap2(W["L0_ffn_WrT"], e * 1024 * 1024, 1024, 1024, 1024)), inc=True)
    if "WsumT" in need:
        out["WsumT"] = ctx.dram("WsumT_bf", [1024, 1024], BF16)
        swp = contextlib.ExitStack()
        ctx._close_after_flush = getattr(ctx, "_close_after_flush", []) + [swp]
        a0 = ctx.sbuf("ws0", [128, 1024], F32, swp, side="right")
        a1 = ctx.sbuf("ws1", [128, 1024], F32, swp, side="right")
        ab = ctx.sbuf("wsb", [128, 1024], BF16, swp, side="right")
        tv = 0
        tq = 0
        for sl in range(8):
            Q.wait(V, tv)
            t0 = Q.op(lambda q, sl=sl: q.dma_start(
                a0[:, :], ap2(W["L0_ffn_WsT"], sl * 128 * 1024, 128, 1024, 1024)),
                inc=True)
            t1 = Q.op(lambda q, sl=sl: q.dma_start(
                a1[:, :], ap2(W["L0_ffn_WsT"], 1024 * 1024 + sl * 128 * 1024,
                              128, 1024, 1024)), inc=True)
            V.wait(Q, t1)
            V.wait(Q, tq)
            tv = V.op(lambda v: v.tensor_tensor(ab[:, :], a0[:, :], a1[:, :],
                                                op=alu.add), inc=True)
            Q.wait(V, tv)
            tq = Q.op(lambda q, sl=sl: q.dma_start(
                ap2(out["WsumT"], sl * 128 * 1024, 128, 1024, 1024), ab[:, :]),
                inc=True)
        out["tok_Q"] = tq
    bg = []
    if "ewall" in need:
        out["ewall"] = ctx.dram("ewall_bf", [8, 3 * 1024 * 1024], BF16)
        for e in range(8):
            for c in range(3):
                bg.append((ap2(out["ewall"], (e * 3 + c) * 1024 * 1024,
                               1024, 1024, 1024),
                           ap2(W["ew_all"], (e * 3 + c) * 1024 * 1024,
                               1024, 1024, 1024)))
    if "w123" in need:
        for k, shp in (("w1", [1024, 4096]), ("w3", [1024, 4096]),
                       ("w2", [4096, 1024])):
            out[k] = ctx.dram(k + "_bf", shp, BF16)
            n = shp[0] * shp[1]
            for c in range(4):
                bg.append((ap2(out[k], c * n // 4, 1024, n // 4096, n // 4096),
                           ap2(W["L1_ffn_" + k], c * n // 4, 1024, n // 4096,
                               n // 4096)))
    out["bg"] = bg
    out["tok"] = tok
    if flush:
        ctx.flush("wprep")
    return out



def emit_l0_ffn(ctx, x_src, x_dst, W, scr, consts, dbg=None, bg=None):
    T, V, Sc, G, Q = ctx.T, ctx.V, ctx.S, ctx.G, ctx.Q
    st = ctx.scope()
    ident = consts["ident"]
    eps_col = consts["eps_col"]

    uT = ctx.sbuf("uT", [128, NCH * S], BF16, st)
    gTsb = ctx.sbuf("gTsb", [16, S], BF16, st)
    gT_dram = dbg["gT"] if dbg else ctx.dram("gT", [16, S], BF16)
    stat_mv = ctx.sbuf("mv", [128, 2 * NT], F32, st)
    stat_rs = ctx.sbuf("rs", [128, NT], F32, st)
    bsq = ctx.sbuf("bsq", [128, D], F32, st)
    cent = ctx.sbuf("cent", [128, NCH * 16], BF16, st)
    brb = ctx.sbuf("brb", [16, D], BF16, st)
    ebias = ctx.sbuf("ebias", [128, 16], F32, st)

    bs1t = ctx.sbuf("bs1t", [128, D], F32, st)
    t_b0 = Q.op(lambda q: q.dma_start(bsq[:, :], W["bs0_bcast"]), inc=True)
    t_b1 = Q.op(lambda q: q.dma_start(bs1t[:, :], W["bs1_bcast"]), inc=True)
    V.wait(Q, t_b1)
    t_bq = V.op(lambda v: v.tensor_tensor(bsq[:, :], bsq[:, :], bs1t[:, :],
                                          op=alu.add), inc=True)
    V.wait(V, t_bq)
    V.op(lambda v: v.tensor_scalar(bsq[:, :], bsq[:, :], 0.25, None, op0=alu.mult))
    t_cent = G.op(lambda g: g.dma_start(
        cent[:, :].rearrange("p (c e) -> p c e", c=NCH),
        W["L0_ffn_centT"].rearrange("(c p) e -> p c e", p=128)), inc=True)
    t_br = G.op(lambda g: g.dma_start(brb[:, :], W["L0_ffn_br"]), inc=True)
    t_eb = Q.op(lambda q: q.dma_start(ebias[:, :], W["ebias_bcast"]), inc=True)

    sA = contextlib.ExitStack()
    xa = [ctx.sbuf("xa", [128, D], F32, sA) for _ in range(2)]
    ua = [ctx.sbuf("ua", [128, D], F32, sA) for _ in range(2)]
    ub = [ctx.sbuf("ub", [128, D], BF16, sA) for _ in range(2)]
    b12 = ctx.sbuf("b12", [128, 12], F32, sA)
    rsc = ctx.sbuf("rsc", [128, 1], F32, sA)
    p_tr = [ctx.psum("ptr", [128, 1024], BF16, sA) for _ in range(2)]
    p_sc = [ctx.psum("psc", [128, 16], F32, sA) for _ in range(2)]
    p_gt = [ctx.psum("pgt", [16, 128], BF16, sA) for _ in range(2)]
    sm = ctx.sbuf("sm", [128, 16], F32, sA)
    sm2 = ctx.sbuf("sm2", [128, 16], F32, sA)
    gcol = ctx.sbuf("gcol", [128, 16], F32, sA)
    gbf = ctx.sbuf("gbf", [128, 16], BF16, sA)
    gtf = ctx.sbuf("gtf", [16, 128], BF16, sA)
    mx = ctx.sbuf("mx", [128, 1], F32, sA)
    rcp = ctx.sbuf("rcp", [128, 1], F32, sA)
    ssum = ctx.sbuf("ssum", [128, 1], F32, sA)

    t_xw = [0, 0]
    t_uaw = [0, 0]
    t_trp = [0, 0]
    t_trev = [0, 0]
    t_scv = [0, 0]
    t_gtv = [0, 0]
    t_gdma = 0

    for t in range(NT):
        par = t % 2
        Q.wait(V, t_xw[par])
        t_x = Q.op(lambda q, t=t, par=par: q.dma_start(
            xa[par][:, :], x_src[t * 128:(t + 1) * 128, :]), inc=True)
        V.wait(Q, t_x)
        t_u = ln_tile(ctx, xa[par], ua[par], b12, stat_mv[:, 2 * t:2 * t + 2],
                      stat_rs[:, t:t + 1], eps_col)
        t_xw[par] = t_u
        Sc.wait(V, t_u)
        Sc.wait(T, t_trp[par])
        t_ub = Sc.op(lambda s, par=par: s.copy(ub[par][:, :], ua[par][:, :]), inc=True)
        t_uaw[par] = t_ub
        T.wait(Sc, t_ub)
        T.wait(V, t_trev[par])
        t_tr = 0
        for c in range(NCH):
            t_tr = T.op(lambda tt, c=c, par=par: tt.transpose(
                p_tr[par][:, c * 128:(c + 1) * 128], ub[par][:, c * 128:(c + 1) * 128],
                ident[:, :]), inc=(c == NCH - 1))
        t_trp[par] = t_tr
        V.wait(T, t_tr)
        t_ev = V.op(lambda v, t=t, par=par: v.tensor_copy(
            uT[:, :].rearrange("p (c s) -> p c s", c=NCH)[:, :, t * 128:(t + 1) * 128],
            p_tr[par][:, :].rearrange("p (c w) -> p c w", c=NCH)), inc=True)
        t_trev[par] = t_ev
        T.wait(V, t_ev)
        if t == 0:
            T.wait(G, t_cent)
        T.wait(V, t_scv[par])
        t_mm = 0
        for c in range(NCH):
            t_mm = T.op(lambda tt, c=c, t=t, par=par: tt.matmul(
                p_sc[par][:, :], uT[:, c * S + t * 128: c * S + (t + 1) * 128],
                cent[:, c * 16:(c + 1) * 16],
                start=(c == 0), stop=(c == NCH - 1)), inc=(c == NCH - 1))
        V.wait(T, t_mm)
        if t == 0:
            V.wait(Q, t_eb)
        t_s = V.op(lambda v, par=par: v.tensor_tensor(sm[:, :], p_sc[par][:, :],
                                                      ebias[:, :], op=alu.add), inc=True)
        t_scv[par] = t_s
        V.wait(V, t_s)
        t_m = V.op(lambda v: v.reduce_max(mx[:, :], sm[:, :], axis=AX, negate=True),
                   inc=True)
        Sc.wait(V, t_m)
        t_e = Sc.op(lambda s: s.activation(sm[:, :], sm[:, :], AF.Exp, bias=mx[:, :],
                                           scale=1.0, accum_out=ssum[:, :]), inc=True)
        V.wait(Sc, t_e)
        t_r = V.op(lambda v: v.reciprocal(rcp[:, :], ssum[:, :]), inc=True)
        V.wait(V, t_r)
        t_p = V.op(lambda v: v.tensor_scalar(sm[:, :], sm[:, :], rcp[:, :], None,
                                             op0=alu.mult), inc=True)
        V.wait(V, t_p)
        t_m1 = V.op(lambda v: v.reduce_max(mx[:, :], sm[:, :], axis=AX), inc=True)
        V.wait(V, t_m1)
        t_ge = V.op(lambda v: v.tensor_scalar(sm2[:, :], sm[:, :], mx[:, :], None,
                                              op0=alu.is_ge), inc=True)
        V.wait(V, t_ge)
        t_s2 = V.op(lambda v: v.scalar_tensor_tensor(sm2[:, :], sm2[:, :], -2.0,
                                                     sm[:, :], op0=alu.mult,
                                                     op1=alu.add), inc=True)
        V.wait(V, t_s2)
        t_m2 = V.op(lambda v: v.reduce_max(mx[:, :], sm2[:, :], axis=AX), inc=True)
        V.wait(V, t_m2)
        t_k = V.op(lambda v: v.tensor_scalar(sm2[:, :], sm[:, :], mx[:, :], None,
                                             op0=alu.is_ge), inc=True)
        V.wait(V, t_k)
        t_g = V.op(lambda v: v.tensor_tensor(gcol[:, :], sm[:, :], sm2[:, :],
                                             op=alu.mult), inc=True)
        V.wait(V, t_g)
        t_gb = V.op(lambda v: v.tensor_copy(gbf[:, :], gcol[:, :]), inc=True)
        T.wait(V, t_gb)
        T.wait(V, t_gtv[par])
        t_gt = T.op(lambda tt, par=par: tt.transpose(p_gt[par][:, :], gbf[:, :],
                                                     ident[:, :]), inc=True)
        V.wait(T, t_gt)
        V.wait(Q, t_gdma)
        t_gtf = V.op(lambda v, par=par: v.tensor_copy(gtf[:, :], p_gt[par][:, :]),
                     inc=True)
        t_gtv[par] = t_gtf
        V.wait(V, t_gtf)
        V.op(lambda v, t=t: v.tensor_copy(gTsb[:, t * 128:(t + 1) * 128], gtf[:, :]))
        Q.wait(V, ctx.V.cnt)
        t_gdma = Q.op(lambda q, t=t: q.dma_start(
            ap2(gT_dram, t * 128, 16, S, 128), gtf[:, :]), inc=True)

    if dbg:
        Q.op(lambda q: q.dma_start(dbg["uT"], uT[:, :]), inc=True)
    ctx.flush("l0ffnA")
    sA.close()

    sB = contextlib.ExitStack()
    p_acc = [ctx.psum("pacc", [128, 1024], F32, sB) for _ in range(4)]
    wrt = [ctx.sbuf("wrt", [128, 8192], BF16, sB) for _ in range(2)]
    gbc = [ctx.sbuf("gbc", [128, 512], BF16, sB) for _ in range(2)]
    gsh = ctx.sbuf("gsh", [128, 512], BF16, sB)
    uge = [ctx.sbuf("uge", [128, NCH * 512], BF16, sB) for _ in range(2)]
    xb = [ctx.sbuf("xb", [128, D], F32, sB) for _ in range(4)]
    ubr = [ctx.sbuf("ubr", [128, D], F32, sB) for _ in range(4)]
    res = [ctx.sbuf("res", [128, D], F32, sB) for _ in range(4)]

    V.op(lambda v: v.memset(gsh[:, :], 0.5), inc=True)
    t_gsh = V.cnt
    NEXP = NEXP_G

    t_mm_par = [0, 0]
    t_gml = [0, 0]
    t_epi = [0, 0, 0, 0]
    t_xbr = [0, 0, 0, 0]
    t_xld = [0, 0, 0, 0]
    t_resq = [0, 0, 0, 0]

    Q.wait(G, scr.get("tok", 0))
    for b in range(NB):
        t_mm_iter = [0] * NEXP
        t_wrt_tok = [0, 0]
        t_gbc_tok = [0, 0]
        for e in range(NEXP):
            par = e % 2
            Q.wait(T, t_mm_par[par])
            if e < 16:
                src = ap3(scr["WrT"], e * 1024 * 1024, 128, 1024, 8, 128 * 1024, 1024)
            else:
                src = ap3(scr["WsumT"], 0, 128, 1024, 8, 128 * 1024, 1024)
            t_wrt_tok[par] = Q.op(lambda q, src=src, par=par: q.dma_start(
                wrt[par][:, :].rearrange("p (c o) -> p c o", c=8), src), inc=True)
            if e < 16:
                G.wait(V, t_gml[par])
                gsrc = bass.AP(gT_dram.tensor, gT_dram.offset + e * S + b * 512,
                               [[0, 128], [1, 512]])
                t_gbc_tok[par] = G.op(lambda g, gsrc=gsrc, par=par: g.dma_start(
                    gbc[par][:, :], gsrc), inc=True)
                if bg:
                    dst_, src_ = bg.pop(0)
                    G.op(lambda g, dst_=dst_, src_=src_: g.dma_start(dst_, src_),
                         inc=True)
                gate_buf = gbc[par]
                t_gate = ("G", t_gbc_tok[par])
            else:
                gate_buf = gsh
                t_gate = ("V", t_gsh)
            V.wait(T, t_mm_par[par])
            if t_gate[0] == "G":
                V.wait(G, t_gate[1])
            else:
                V.wait(V, t_gate[1])
            t_gm = 0
            for c in range(NCH):
                t_gm = V.op(lambda v, c=c, par=par, b=b, gate_buf=gate_buf:
                            v.tensor_tensor(uge[par][:, c * 512:(c + 1) * 512],
                                            uT[:, c * S + b * 512: c * S + (b + 1) * 512],
                                            gate_buf[:, :], op=alu.mult),
                            inc=(c == NCH - 1))
            t_gml[par] = t_gm
            T.wait(V, t_gm)
            T.wait(Q, t_wrt_tok[par])
            if e == 0:
                for t4 in range(4):
                    T.wait(V, t_epi[t4])
            t_mm = 0
            for t4 in range(4):
                for h in range(2):
                    for k in range(NCH):
                        t_mm = T.op(lambda tt, t4=t4, h=h, k=k, par=par, e=e: tt.matmul(
                            p_acc[t4][:, h * 512:(h + 1) * 512],
                            uge[par][:, k * 512 + t4 * 128: k * 512 + (t4 + 1) * 128],
                            wrt[par][:, k * 1024 + h * 512: k * 1024 + (h + 1) * 512],
                            start=(e == 0 and k == 0), stop=False),
                            inc=(t4 == 3 and h == 1 and k == NCH - 1))
            t_mm_par[par] = t_mm
        if b == 0:
            T.wait(G, t_br)
        t_bias = 0
        for t4 in range(4):
            ti = b * 4 + t4
            for h in range(2):
                t_bias = T.op(lambda tt, ti=ti, h=h, t4=t4: tt.matmul(
                    p_acc[t4][:, h * 512:(h + 1) * 512],
                    gTsb[:, ti * 128:(ti + 1) * 128],
                    brb[:, h * 512:(h + 1) * 512],
                    start=False, stop=True), inc=(t4 == 3 and h == 1))
        for t4 in range(4):
            ti = b * 4 + t4
            if dbg and "rs" in dbg:
                V.wait(T, t_bias)
                t_rc = V.op(lambda v, t4=t4: v.tensor_copy(res[t4][:, :],
                                                           p_acc[t4][:, :]), inc=True)
                Q.wait(V, t_rc)
                t_resq[t4] = Q.op(lambda q, ti=ti, t4=t4: q.dma_start(
                    dbg["rs"][ti * 128:(ti + 1) * 128, :], res[t4][:, :]), inc=True)
            Q.wait(V, t_xbr[t4])
            t_x = Q.op(lambda q, ti=ti, t4=t4: q.dma_start(
                xb[t4][:, :], x_src[ti * 128:(ti + 1) * 128, :]), inc=True)
            t_xld[t4] = t_x
            V.wait(Q, t_x)
            V.wait(T, t_bias)
            t_u2 = V.op(lambda v, ti=ti, t4=t4: v.tensor_scalar(
                ubr[t4][:, :], xb[t4][:, :], stat_mv[:, 2 * ti:2 * ti + 1],
                stat_rs[:, ti:ti + 1], op0=alu.subtract, op1=alu.mult), inc=True)
            V.wait(V, t_u2)
            V.wait(Q, t_resq[t4])
            t_c1 = V.op(lambda v, t4=t4: v.tensor_tensor(
                res[t4][:, :], ubr[t4][:, :], p_acc[t4][:, :], op=alu.add), inc=True)
            t_epi[t4] = t_c1
            V.wait(V, t_c1)
            t_c2 = V.op(lambda v, t4=t4: v.scalar_tensor_tensor(
                res[t4][:, :], res[t4][:, :], 0.5, xb[t4][:, :],
                op0=alu.mult, op1=alu.add), inc=True)
            t_xbr[t4] = t_c2
            V.wait(V, t_c2)
            t_c3 = V.op(lambda v, t4=t4: v.tensor_tensor(
                res[t4][:, :], res[t4][:, :], bsq[:, :], op=alu.add), inc=True)
            Q.wait(V, t_c3)
            t_resq[t4] = Q.op(lambda q, ti=ti, t4=t4: q.dma_start(
                x_dst[ti * 128:(ti + 1) * 128, :], res[t4][:, :]), inc=True)

    if bg:
        for dst_, src_ in bg:
            G.op(lambda g, dst_=dst_, src_=src_: g.dma_start(dst_, src_), inc=True)
        del bg[:]
    ctx.flush("l0ffnB")
    sB.close()
    st.close()



def emit_ln_phaseA(ctx, x_src, uT, consts, sA, stats=None, hook=None):
    T, V, Sc, Q = ctx.T, ctx.V, ctx.S, ctx.Q
    ident = consts["ident"]
    eps_col = consts["eps_col"]
    xa = [ctx.sbuf("xa", [128, D], F32, sA) for _ in range(2)]
    ua = [ctx.sbuf("ua", [128, D], F32, sA) for _ in range(2)]
    ub = [ctx.sbuf("ub", [128, D], BF16, sA) for _ in range(2)]
    b12 = ctx.sbuf("b12", [128, 12], F32, sA)
    p_tr = [ctx.psum("ptr", [128, 1024], BF16, sA) for _ in range(2)]
    if stats is None:
        mvt = ctx.sbuf("mvt", [128, 2], F32, sA)
        rst = ctx.sbuf("rst", [128, 1], F32, sA)
    t_xw = [0, 0]
    t_trp = [0, 0]
    t_trev = [0, 0]
    for t in range(NT):
        par = t % 2
        Q.wait(V, t_xw[par])
        t_x = Q.op(lambda q, t=t, par=par: q.dma_start(
            xa[par][:, :], x_src[t * 128:(t + 1) * 128, :]), inc=True)
        V.wait(Q, t_x)
        if stats is not None:
            mv = stats[0][:, 2 * t:2 * t + 2]
            rs = stats[1][:, t:t + 1]
        else:
            mv = mvt[:, :]
            rs = rst[:, :]
        t_u = ln_tile(ctx, xa[par], ua[par], b12, mv, rs, eps_col)
        t_xw[par] = t_u
        Sc.wait(V, t_u)
        Sc.wait(T, t_trp[par])
        t_ub = Sc.op(lambda s, par=par: s.copy(ub[par][:, :], ua[par][:, :]), inc=True)
        T.wait(Sc, t_ub)
        T.wait(V, t_trev[par])
        t_tr = 0
        for c in range(NCH):
            t_tr = T.op(lambda tt, c=c, par=par: tt.transpose(
                p_tr[par][:, c * 128:(c + 1) * 128], ub[par][:, c * 128:(c + 1) * 128],
                ident[:, :]), inc=(c == NCH - 1))
        t_trp[par] = t_tr
        V.wait(T, t_tr)
        t_ev = V.op(lambda v, t=t, par=par: v.tensor_copy(
            uT[:, :].rearrange("p (c s) -> p c s", c=NCH)[:, :, t * 128:(t + 1) * 128],
            p_tr[par][:, :].rearrange("p (c w) -> p c w", c=NCH)), inc=True)
        t_trev[par] = t_ev
        if hook is not None:
            hook(t, t_ev)



def emit_l0_attn(ctx, x_src, x_dst, W, consts):
    T, V, Sc, G, Q = ctx.T, ctx.V, ctx.S, ctx.G, ctx.Q
    st = ctx.scope()
    ident = consts["ident"]
    uT = ctx.sbuf("uT", [128, NCH * S], BF16, st)
    o_dram = ctx.dram("o0T_scr", [D, S], BF16)

    gc = ctx.sbuf("gc0", [64, S], BF16, st)
    gs = ctx.sbuf("gs0", [64, S], BF16, st)
    mask0T = ctx.sbuf("m0T", [128, 256], F32, st)
    Q.op(lambda q: q.dma_start(gc[:, :], W["rope_gc"][:, 0:S]), inc=True)
    Q.op(lambda q: q.dma_start(gs[:, :], W["rope_gs"][:, 0:S]), inc=True)
    t_consts = Q.op(lambda q: q.dma_start(mask0T[:, :], W["mask_l0T"]), inc=True)

    sA = contextlib.ExitStack()
    emit_ln_phaseA(ctx, x_src, uT, consts, sA)
    ctx.flush("l0attnA")
    sA.close()

    sB = contextlib.ExitStack()
    wsl3 = [[ctx.sbuf("wsl0", [128, NCH * 128], BF16, sB) for _ in range(3)]
            for _ in range(2)]
    qa = ctx.sbuf("qa0", [64, S], BF16, sB)
    rt = ctx.sbuf("rt0", [64, 1024], BF16, sB)
    t1b = ctx.sbuf("t1b0", [64, 1024], BF16, sB)
    qf = [ctx.sbuf("qf0", [64, S], BF16, sB) for _ in range(2)]
    kf = [ctx.sbuf("kf0", [64, S], BF16, sB) for _ in range(2)]
    v2e = ctx.sbuf("v2e0", [128, NT * 132], BF16, sB)
    o_hgT = ctx.sbuf("ohgT0", [128, S], BF16, sB)
    atT = [ctx.sbuf("atT0", [128, 256], BF16, sB) for _ in range(4)]
    smd = [ctx.sbuf("smd0", [128, 256], F32, sB) for _ in range(2)]
    sq2 = ctx.sbuf("sq20", [64, S], BF16, sB)
    on64 = ctx.sbuf("on640", [64, 1], BF16, sB)
    on1r = ctx.sbuf("on1r0", [1, 128], F32, sB)
    nrm = ctx.sbuf("nrm0", [1, S], F32, sB)
    mk2 = ctx.sbuf("mk20", [1, 1], F32, sB)
    mq2 = ctx.sbuf("mq20", [1, 1], F32, sB)
    negBr = ctx.sbuf("negBr0", [1, 2], F32, sB)
    negB = ctx.sbuf("negB0", [128, 2], F32, sB)
    osum = ctx.sbuf("osum0", [1, S], F32, sB)
    mx80 = ctx.sbuf("mx80", [1, 8], F32, sB)
    p_qv = ctx.psum("pqv0", [128, 512], F32, sB)
    p_qv2 = ctx.psum("pqv0b", [128, 512], F32, sB)
    p_s = [ctx.psum("ps0", [128, 256], F32, sB) for _ in range(2)]
    p_oT = [ctx.psum("poT0", [65, 128], F32, sB) for _ in range(2)]
    p_rb = ctx.psum("prb0", [64, 512], F32, sB)

    t_on = V.op(lambda v: v.memset(on64[:, :], 1.0), inc=True)
    V.op(lambda v: v.memset(on1r[:, :], 1.0))
    V.op(lambda v: v.memset(v2e[:, :], 1.0))
    t_v2e1 = V.cnt
    T.wait(V, t_on)

    t_hgT = 0
    t_pq_ev = 0
    t_pv_ev = 0
    t_nmL = [0]
    t_nev = 0
    t_nb = 0
    t_nbG = 0
    t_nbS = [0, 0]
    t_rope_last = 0
    t_rbT_h = [0, 0]
    t_exp = [0, 0]
    t_dgV = [0, 0]
    t_avm = [0, 0, 0, 0]
    t_smdS = [0, 0]
    t_oevS = [0, 0]
    t_rcpV = [0, 0]
    t_rbV = [0]
    t_oevL = [0]
    t_pq_par = [0, 0]
    t_nq = [0, 0]
    t_vx = [0]
    pnA = [None]
    pnB = [None]

    V.wait(Q, t_consts)
    for hg in range(8):
        par = hg % 2
        G.wait(T, t_hgT)
        for r, region in enumerate((0, 1024, 2048)):
            srcw = ap3(W["L0_attn_W_qkv"], region + hg * 128,
                       128, 3072, NCH, 128 * 3072, 128)
            t_w = G.op(lambda g, r=r, par=par, srcw=srcw: g.dma_start(
                wsl3[par][r][:, :].rearrange("p (k m) -> p k m", k=NCH), srcw),
                inc=True)
        T.wait(G, t_w)
        for which in range(2):
            for hoff in range(2):
                dstf = qf[hoff] if which == 0 else kf[hoff]
                for blk in range(S // 512):
                    bq = blk % 2
                    pv = p_qv if bq == 0 else p_qv2
                    if which == 0 and hoff == 1 and blk == 0 and pnB[0] is not None:
                        pnB[0]()
                        pnB[0] = None
                    T.wait(Sc, t_pq_par[bq])
                    T.wait(V, t_nq[bq])
                    t_mm = 0
                    for k in range(NCH):
                        t_mm = T.op(lambda tt, k=k, blk=blk, which=which, hoff=hoff,
                                    par=par, pv=pv: tt.matmul(
                            pv[0:64, :],
                            wsl3[par][which][:, k * 128 + hoff * 64:
                                             k * 128 + hoff * 64 + 64],
                            uT[:, k * S + blk * 512:k * S + (blk + 1) * 512],
                            start=(k == 0), stop=(k == NCH - 1)), inc=(k == NCH - 1))
                    Sc.wait(T, t_mm)
                    if blk == 0:
                        Sc.wait(V, t_rope_last)
                    sc_ = 0.125 if which == 0 else 1.0
                    t_pq_ev = Sc.op(lambda s, blk=blk, sc_=sc_, pv=pv: s.activation(
                        qa[:, blk * 512:(blk + 1) * 512], pv[0:64, :], AF.Copy,
                        bias=0.0, scale=sc_), inc=True)
                    t_pq_par[bq] = t_pq_ev
                V.wait(Sc, t_pq_ev)
                V.wait(T, t_hgT)
                SL = min(1024, S)
                for sl in range(S // SL):
                    c0 = sl * SL
                    t_r1 = V.op(lambda v, c0=c0, SL=SL: v.tensor_scalar(
                        rt[0:32, 0:SL], qa[32:64, c0:c0 + SL], -1.0, None,
                        op0=alu.mult), inc=True)
                    t_r2 = V.op(lambda v, c0=c0, SL=SL: v.tensor_copy(
                        rt[32:64, 0:SL], qa[0:32, c0:c0 + SL]), inc=True)
                    t_t1 = V.op(lambda v, c0=c0, SL=SL: v.tensor_tensor(
                        t1b[:, 0:SL], qa[:, c0:c0 + SL], gc[:, c0:c0 + SL],
                        op=alu.mult), inc=True)
                    V.wait(V, t_r2)
                    t_t2 = V.op(lambda v, c0=c0, SL=SL: v.tensor_tensor(
                        rt[:, 0:SL], rt[:, 0:SL], gs[:, c0:c0 + SL], op=alu.mult),
                        inc=True)
                    V.wait(V, t_t2)
                    t_rope_last = V.op(lambda v, c0=c0, SL=SL, dstf=dstf:
                                       v.tensor_tensor(
                        dstf[0:64, c0:c0 + SL], t1b[:, 0:SL], rt[:, 0:SL],
                        op=alu.add), inc=True)
        t_qk = t_rope_last
        Sc.wait(V, t_qk)
        T.wait(Sc, t_pq_ev)
        for h in range(2):
            for which, srcf in ((1, kf[h]), (0, qf[h])):
                Sc.wait(T, t_nmL[0])
                for blk in range(S // 512):
                    bq = blk % 2
                    pv = p_qv if bq == 0 else p_qv2
                    t_sqb = Sc.op(lambda s, srcf=srcf, blk=blk: s.activation(
                        sq2[:, blk * 512:(blk + 1) * 512],
                        srcf[0:64, blk * 512:(blk + 1) * 512], AF.Square),
                        inc=True)
                    T.wait(Sc, t_sqb)
                    T.wait(V, t_nq[bq])
                    t_nm = T.op(lambda tt, blk=blk, pv=pv: tt.matmul(
                        pv[0:1, :], on64[:, :], sq2[:, blk * 512:(blk + 1) * 512],
                        start=True, stop=True), inc=True)
                    t_nmL[0] = t_nm
                    V.wait(T, t_nm)
                    V.wait(V, t_vx[0])
                    t_rm = V.op(lambda v, blk=blk, pv=pv: v.reduce_max(
                        mx80[0:1, blk:blk + 1], pv[0:1, :], axis=AX), inc=True)
                    t_vx[0] = t_rm
                    t_nq[bq] = t_rm
                dst_m = mk2 if which == 1 else mq2
                V.wait(V, t_vx[0])
                t_mk = V.op(lambda v, dst_m=dst_m: v.reduce_max(
                    dst_m[:, :], mx80[0:1, 0:S // 512], axis=AX), inc=True)
                t_vx[0] = t_mk
                V.wait(V, t_mk)
            V.wait(Sc, t_nbS[h])
            t_b1 = V.op(lambda v, h=h: v.tensor_scalar(
                negBr[:, h:h + 1], mq2[:, :], mk2[0:1, 0:1], None,
                op0=alu.mult), inc=True)
            Sc.wait(V, t_b1)
            t_b2 = Sc.op(lambda s, h=h: s.activation(
                negBr[:, h:h + 1], negBr[:, h:h + 1], AF.Ln,
                bias=consts["epsb"][0:1, 0:1], scale=1.0), inc=True)
            Sc.wait(Sc, t_b2)
            t_b3 = Sc.op(lambda s, h=h: s.activation(
                negBr[:, h:h + 1], negBr[:, h:h + 1], AF.Exp, scale=0.5), inc=True)
            V.wait(Sc, t_b3)
            t_nb = V.op(lambda v, h=h: v.tensor_scalar(
                negBr[:, h:h + 1], negBr[:, h:h + 1], -1.0, None, op0=alu.mult),
                inc=True)
        T.wait(V, t_nb)
        T.wait(V, t_nq[0])
        t_nbm = T.op(lambda tt: tt.matmul(p_qv[:, 0:2], on1r[0:1, :], negBr[:, :],
                                          start=True, stop=True), inc=True)
        V.wait(T, t_nbm)
        V.wait(Sc, t_nbS[0])
        V.wait(Sc, t_nbS[1])
        t_nbG = V.op(lambda v: v.tensor_copy(negB[:, :], p_qv[:, 0:2]), inc=True)
        Sc.wait(V, t_nbG)
        V.wait(T, t_hgT)
        if hg == 0:
            V.wait(V, t_v2e1)
        for t in range(NT):
            tq = t % 2
            pv = p_qv if tq == 0 else p_qv2
            T.wait(V, t_nq[tq])
            if t == 0:
                T.wait(V, t_nbG)
            t_mm = 0
            for k in range(NCH):
                t_mm = T.op(lambda tt, k=k, t=t, par=par, pv=pv: tt.matmul(
                    pv[:, 0:128],
                    uT[:, k * S + t * 128:k * S + (t + 1) * 128],
                    wsl3[par][2][:, k * 128:(k + 1) * 128],
                    start=(k == 0), stop=(k == NCH - 1)), inc=(k == NCH - 1))
            V.wait(T, t_mm)
            V.op(lambda v, t=t, pv=pv: v.tensor_copy(
                v2e[:, t * 132:t * 132 + 64], pv[:, 0:64]))
            t_pv_ev = V.op(lambda v, t=t, pv=pv: v.tensor_copy(
                v2e[:, t * 132 + 65:t * 132 + 129], pv[:, 64:128]), inc=True)
            t_nq[tq] = t_pv_ev
        for h in range(2):
            bias_ap = negB[:, h:h + 1]
            os_h = osum if h == 0 else nrm

            def do_av(pe, h=h, os_h=os_h):
                w, t_e_w = pe
                wp = w % 2
                T.wait(Sc, t_e_w)
                T.wait(Sc, t_oevS[wp])
                t_av = 0
                if w > 0:
                    t_av = T.op(lambda tt, h=h, w=w: tt.matmul(
                        p_oT[w % 2][:, :],
                        v2e[:, (w - 1) * 132 + h * 65:(w - 1) * 132 + h * 65 + 65],
                        atT[(w - 1) % 4][:, 128:256],
                        start=True, stop=False), inc=False)
                t_av = T.op(lambda tt, h=h, w=w: tt.matmul(
                    p_oT[w % 2][:, :],
                    v2e[:, w * 132 + h * 65:w * 132 + h * 65 + 65],
                    atT[w % 4][:, 0:128],
                    start=(w == 0), stop=True), inc=True)
                if w > 0:
                    t_avm[(w - 1) % 4] = t_av
                if w == NT - 1:
                    t_avm[w % 4] = t_av
                Sc.wait(T, t_av)
                if w == 0:
                    Sc.wait(T, t_rbT_h[h])
                Sc.op(lambda s, h=h, w=w, wp=wp: s.copy(
                    o_hgT[h * 64:h * 64 + 64, w * 128:(w + 1) * 128],
                    p_oT[wp][0:64, :]))
                t_ob = Sc.op(lambda s, h=h, w=w, wp=wp, os_h=os_h: s.copy(
                    os_h[0:1, w * 128:(w + 1) * 128], p_oT[wp][64:65, :]),
                    inc=True)
                t_oevS[wp] = t_ob
                return t_av, t_ob

            pends = []
            t_av = t_ob = 0
            for j in range(NT):
                if h == 1 and j == 3 and pnA[0] is not None:
                    pnA[0]()
                    pnA[0] = None
                sp = j % 2
                nq = 256 if j < NT - 1 else 128
                T.wait(Sc, t_exp[sp])
                T.wait(V, t_dgV[sp])
                if j == 0 and h == 0 and hg == 0:
                    T.wait(V, t_qk)
                    T.wait(V, t_pv_ev)
                t_mm = T.op(lambda tt, h=h, j=j, nq=nq, sp=sp: tt.matmul(
                    p_s[sp][:, 0:nq],
                    kf[h][0:64, j * 128:(j + 1) * 128],
                    qf[h][0:64, j * 128:j * 128 + nq],
                    start=True, stop=True), inc=True)
                V.wait(T, t_mm)
                V.wait(Sc, t_exp[sp])
                V.wait(Sc, t_smdS[sp])
                t_md = V.op(lambda v, sp=sp, nq=nq: v.tensor_tensor(
                    smd[sp][:, 0:nq], p_s[sp][:, 0:nq], mask0T[:, 0:nq],
                    op=alu.add), inc=True)
                t_dgV[sp] = t_md
                Sc.wait(V, t_md)
                Sc.wait(T, t_avm[j % 4])
                t_e = Sc.op(lambda s, sp=sp, j=j, nq=nq, bias_ap=bias_ap:
                            s.activation(
                    atT[j % 4][:, 0:nq], smd[sp][:, 0:nq], AF.Exp,
                    bias=bias_ap, scale=1.0), inc=True)
                t_smdS[sp] = t_e
                t_exp[sp] = t_e
                t_nbS[h] = t_e
                pends.append((j, t_e))
                if len(pends) > 2:
                    t_av, t_ob = do_av(pends.pop(0))
            while pends:
                t_av, t_ob = do_av(pends.pop(0))
            def mk_bn(h=h, os_h=os_h, t_ob=t_ob):
                def go():
                    V.wait(Sc, t_ob)
                    V.wait(T, t_rbT_h[h])
                    t_on_ = 0
                    for blk in range(S // 512):
                        t_ri = V.op(lambda v, blk=blk, os_h=os_h: v.reciprocal(
                            os_h[0:1, blk * 512:(blk + 1) * 512],
                            os_h[0:1, blk * 512:(blk + 1) * 512]), inc=True)
                        T.wait(V, t_ri)
                        T.wait(V, t_rbV[0])
                        t_rb = T.op(lambda tt, blk=blk, os_h=os_h: tt.matmul(
                            p_rb[:, :], on1r[0:1, 0:64],
                            os_h[0:1, blk * 512:(blk + 1) * 512],
                            start=True, stop=True), inc=True)
                        t_rbT_h[h] = t_rb
                        V.wait(T, t_rb)
                        t_on_ = V.op(lambda v, h=h, blk=blk: v.tensor_tensor(
                            o_hgT[h * 64:h * 64 + 64, blk * 512:(blk + 1) * 512],
                            o_hgT[h * 64:h * 64 + 64, blk * 512:(blk + 1) * 512],
                            p_rb[0:64, :], op=alu.mult), inc=True)
                        t_rbV[0] = t_on_
                    t_rcpV[h] = t_on_
                    t_oevL[0] = t_on_
                return go
            if h == 0:
                pnA[0] = mk_bn()
            else:
                bn1 = mk_bn()

                def mk_bn_dma(bn1=bn1, hg=hg):
                    def go():
                        bn1()
                        Q.wait(ctx.V, t_oevL[0])
                        t_odma = Q.op(lambda q, hg=hg: q.dma_start(
                            ap2(o_dram, hg * 128 * S, 128, S, S), o_hgT[:, :]),
                            inc=True)
                        V.wait(Q, t_odma)
                    return go
                pnB[0] = mk_bn_dma()
        t_hgT = t_av
    if pnB[0] is not None:
        pnB[0]()
        pnB[0] = None
    ctx.flush("l0attnB")
    sB.close()

    sC = contextlib.ExitStack()
    wo = ctx.sbuf("wo0", [128, NCH * D], BF16, sC)
    ot = [ctx.sbuf("ot0", [128, D], BF16, sC) for _ in range(2)]
    xb = [ctx.sbuf("xb0", [128, D], F32, sC) for _ in range(2)]
    res = [ctx.sbuf("res0", [128, D], F32, sC) for _ in range(2)]
    p_o = [ctx.psum("po0", [128, 1024], F32, sC) for _ in range(2)]
    t_wo = G.op(lambda g: g.dma_start(
        wo[:, :].rearrange("p (k m) -> p k m", k=NCH),
        ap3(W["L0_attn_W_out"], 0, 128, 1024, NCH, 128 * 1024, 1024)), inc=True)
    t_xbr = [0, 0]
    t_st = [0, 0]
    t_po = [0, 0]
    t_otm = [0, 0]
    for t in range(NT):
        par = t % 2
        Q.wait(V, t_xbr[par])
        t_x = Q.op(lambda q, t=t, par=par: q.dma_start(
            xb[par][:, :], x_src[t * 128:(t + 1) * 128, :]), inc=True)
        Q.wait(T, t_otm[par])
        t_ol = Q.op(lambda q, t=t, par=par: q.dma_start(
            ot[par][:, :].rearrange("p (c m) -> p c m", c=NCH),
            ap3(o_dram, t * 128, 128, S, NCH, 128 * S, 128)), inc=True)
        if t == 0:
            T.wait(G, t_wo)
        T.wait(Q, t_ol)
        T.wait(V, t_po[par])
        t_mm = 0
        for k in range(NCH):
            T.op(lambda tt, k=k, par=par: tt.matmul(
                p_o[par][:, 0:512], ot[par][:, k * 128:(k + 1) * 128],
                wo[:, k * 1024:k * 1024 + 512],
                start=(k == 0), stop=(k == NCH - 1)))
            t_mm = T.op(lambda tt, k=k, par=par: tt.matmul(
                p_o[par][:, 512:1024], ot[par][:, k * 128:(k + 1) * 128],
                wo[:, k * 1024 + 512:(k + 1) * 1024],
                start=(k == 0), stop=(k == NCH - 1)), inc=(k == NCH - 1))
        t_otm[par] = t_mm
        V.wait(T, t_mm)
        V.wait(Q, t_x)
        V.wait(Q, t_st[par])
        t_r = V.op(lambda v, par=par: v.tensor_tensor(
            res[par][:, :], xb[par][:, :], p_o[par][:, :], op=alu.add), inc=True)
        t_po[par] = t_r
        t_xbr[par] = t_r
        Q.wait(V, t_r)
        t_st[par] = Q.op(lambda q, t=t, par=par: q.dma_start(
            x_dst[t * 128:(t + 1) * 128, :], res[par][:, :]), inc=True)
    ctx.flush("l0attnC")
    sC.close()
    st.close()


def build_l0_attn_program():
    nc = bass.Bass("TRN2", target_bir_lowering=False)
    stack = contextlib.ExitStack()
    x_in = nc.dram_tensor("x_in", [S, D], F32, kind="ExternalInput").ap()
    x_out = nc.dram_tensor("x_out", [S, D], F32, kind="ExternalOutput").ap()
    aps = {}
    aps["L0_attn_W_qkv"] = nc.dram_tensor("L0_attn_W_qkv", [1024, 3072], F32,
                                          kind="ExternalInput").ap()
    aps["L0_attn_W_out"] = nc.dram_tensor("L0_attn_W_out", [1024, 1024], F32,
                                          kind="ExternalInput").ap()
    for k in ("rope_gc", "rope_gs"):
        aps[k] = nc.dram_tensor(k, [64, 4096], BF16, kind="ExternalInput").ap()
    aps["mask_l0T"] = nc.dram_tensor("mask_l0T", [128, 256], F32,
                                     kind="ExternalInput").ap()
    aps["c_ident"] = nc.dram_tensor("c_ident", [128, 128], BF16,
                                    kind="ExternalInput").ap()
    with stack:
        ctx = Ctx(nc, stack)
        consts = make_consts(ctx, aps)
        ctx.flush("init")
        emit_l0_attn(ctx, x_in, x_out, aps, consts)
        ctx.flush("fin")
    return nc



def emit_l1_attn(ctx, x_src, x_dst, W, consts):
    T, V, Sc, G, Q = ctx.T, ctx.V, ctx.S, ctx.G, ctx.Q
    st = ctx.scope()
    uT = ctx.sbuf("uT", [128, NCH * S], BF16, st)
    ghT = ctx.sbuf("ghT", [128, NCH * S], BF16, st)

    sA = contextlib.ExitStack()
    emit_ln_phaseA(ctx, x_src, uT, consts, sA)
    ctx.flush("l1attnA")
    sA.close()

    sB = contextlib.ExitStack()
    NBK = S // 512
    bf_f = ctx.sbuf("bf_f", [128, S], F32, sB)
    bf_i = ctx.sbuf("bf_i", [128, S], F32, sB)
    bf_h = ctx.sbuf("bf_h", [128, S], F32, sB)
    om = [ctx.sbuf("om", [128, 512], F32, sB) for _ in range(2)]
    wf = [ctx.sbuf("wf", [128, 1024], BF16, sB) for _ in range(2)]
    wi = [ctx.sbuf("wi", [128, 1024], BF16, sB) for _ in range(2)]
    wg = [ctx.sbuf("wg", [128, 1024], BF16, sB) for _ in range(2)]
    p_fi = [ctx.psum("pfi", [128, 512], F32, sB) for _ in range(2)]
    p_g = [ctx.psum("pg", [128, 512], F32, sB) for _ in range(2)]

    t_w = [0, 0]
    t_mmW = [0, 0]
    t_fi_ev = [0, 0]
    t_g_ev = [0, 0]
    t_scan = 0
    t_gh = 0

    def wslice(wT_ap, cc):
        return ap3(wT_ap, cc * 128, 128, 1024, NCH, 128 * 1024, 128)

    for cc in range(NCH):
        par = cc % 2
        t_scan_prev = t_scan
        G.wait(T, t_mmW[par])
        G.op(lambda g, cc=cc: g.dma_start(
            wf[cc % 2][:, :].rearrange("p (k m) -> p k m", k=NCH),
            wslice(W["L1_attn_W_fT"], cc)), inc=True)
        G.op(lambda g, cc=cc: g.dma_start(
            wi[cc % 2][:, :].rearrange("p (k m) -> p k m", k=NCH),
            wslice(W["L1_attn_W_iT"], cc)), inc=True)
        t_w[par] = G.op(lambda g, cc=cc: g.dma_start(
            wg[cc % 2][:, :].rearrange("p (k m) -> p k m", k=NCH),
            wslice(W["L1_attn_W_gT"], cc)), inc=True)
        T.wait(G, t_w[par])
        Sc.wait(V, t_scan_prev)
        for which, wbuf, act in (("f", wf, AF.Sigmoid), ("i", wi, AF.Silu)):
            for blk in range(NBK):
                bp = blk % 2
                T.wait(Sc, t_fi_ev[bp])
                t_mm = 0
                for k in range(NCH):
                    t_mm = T.op(lambda tt, k=k, blk=blk, bp=bp, wbuf=wbuf, par=par:
                                tt.matmul(p_fi[bp][:, :],
                                          wbuf[par][:, k * 128:(k + 1) * 128],
                                          uT[:, k * S + blk * 512:k * S + (blk + 1) * 512],
                                          start=(k == 0), stop=(k == NCH - 1)),
                                inc=(k == NCH - 1))
                t_mmW[par] = t_mm
                Sc.wait(T, t_mm)
                dst = bf_f if which == "f" else bf_i
                t_fi_ev[bp] = Sc.op(lambda s, blk=blk, bp=bp, dst=dst, act=act:
                                    s.activation(dst[:, blk * 512:(blk + 1) * 512],
                                                 p_fi[bp][:, :], act), inc=True)
        t_inp = 0
        for blk in range(NBK):
            bp = blk % 2
            V.wait(Sc, t_fi_ev[0])
            V.wait(Sc, t_fi_ev[1])
            t_om = V.op(lambda v, blk=blk, bp=bp: v.tensor_scalar(
                om[bp][:, :], bf_f[:, blk * 512:(blk + 1) * 512], -1.0, 1.0,
                op0=alu.mult, op1=alu.add), inc=True)
            V.wait(V, t_om)
            t_inp = V.op(lambda v, blk=blk, bp=bp: v.tensor_tensor(
                bf_i[:, blk * 512:(blk + 1) * 512],
                bf_i[:, blk * 512:(blk + 1) * 512], om[bp][:, :], op=alu.mult),
                inc=True)
        V.wait(V, t_inp)
        t_scan = V.op(lambda v: v.tensor_tensor_scan(
            bf_h[:, :], bf_f[:, :], bf_i[:, :], 0.0,
            op0=alu.mult, op1=alu.add), inc=True)
        for blk in range(NBK):
            bp = blk % 2
            T.wait(V, t_g_ev[bp])
            t_mm = 0
            for k in range(NCH):
                t_mm = T.op(lambda tt, k=k, blk=blk, bp=bp, par=par:
                            tt.matmul(p_g[bp][:, :],
                                      wg[par][:, k * 128:(k + 1) * 128],
                                      uT[:, k * S + blk * 512:k * S + (blk + 1) * 512],
                                      start=(k == 0), stop=(k == NCH - 1)),
                            inc=(k == NCH - 1))
            t_mmW[par] = t_mm
            V.wait(T, t_mm)
            V.wait(V, t_scan)
            t_g_ev[bp] = V.op(lambda v, cc=cc, blk=blk, bp=bp: v.tensor_tensor(
                ghT[:, cc * S + blk * 512:cc * S + (blk + 1) * 512],
                p_g[bp][:, :], bf_h[:, blk * 512:(blk + 1) * 512], op=alu.mult),
                inc=True)
        t_gh = t_g_ev[1]
    ctx.flush("l1attnB")
    sB.close()

    sC = contextlib.ExitStack()
    wo = ctx.sbuf("wo", [128, NCH * D], BF16, sC)
    xb = [ctx.sbuf("xb", [128, D], F32, sC) for _ in range(2)]
    res = [ctx.sbuf("res", [128, D], F32, sC) for _ in range(2)]
    p_o = [ctx.psum("po", [128, 1024], F32, sC) for _ in range(2)]
    t_wo = G.op(lambda g: g.dma_start(
        wo[:, :].rearrange("p (k m) -> p k m", k=NCH),
        ap3(W["L1_attn_W_oT"], 0, 128, 1024, NCH, 128 * 1024, 1024)), inc=True)
    t_xbr = [0, 0]
    t_st = [0, 0]
    t_po = [0, 0]
    for t in range(NT):
        par = t % 2
        Q.wait(V, t_xbr[par])
        t_x = Q.op(lambda q, t=t, par=par: q.dma_start(
            xb[par][:, :], x_src[t * 128:(t + 1) * 128, :]), inc=True)
        if t == 0:
            T.wait(G, t_wo)
        T.wait(V, t_po[par])
        t_mm = 0
        for k in range(NCH):
            t_mm = T.op(lambda tt, k=k, t=t, par=par: tt.matmul(
                p_o[par][:, 0:512], ghT[:, k * S + t * 128:k * S + (t + 1) * 128],
                wo[:, k * 1024:k * 1024 + 512],
                start=(k == 0), stop=(k == NCH - 1)), inc=False)
            t_mm = T.op(lambda tt, k=k, t=t, par=par: tt.matmul(
                p_o[par][:, 512:1024], ghT[:, k * S + t * 128:k * S + (t + 1) * 128],
                wo[:, k * 1024 + 512:(k + 1) * 1024],
                start=(k == 0), stop=(k == NCH - 1)), inc=(k == NCH - 1))
        V.wait(T, t_mm)
        V.wait(Q, t_x)
        V.wait(Q, t_st[par])
        t_r = V.op(lambda v, par=par: v.tensor_tensor(
            res[par][:, :], xb[par][:, :], p_o[par][:, :], op=alu.add), inc=True)
        t_po[par] = t_r
        t_xbr[par] = t_r
        Q.wait(V, t_r)
        t_st[par] = Q.op(lambda q, t=t, par=par: q.dma_start(
            x_dst[t * 128:(t + 1) * 128, :], res[par][:, :]), inc=True)
    ctx.flush("l1attnC")
    sC.close()
    st.close()



def rms_cols(ctx, src_sb, sq_scr, ssq, rstd_col, eps_col):
    Sc = ctx.S
    t_sq = Sc.op(lambda s: s.activation(sq_scr[:, :], src_sb[:, :], AF.Square,
                                        accum_out=ssq), inc=True)
    Sc.wait(Sc, t_sq)
    t_ln = Sc.op(lambda s: s.activation(rstd_col, ssq, AF.Ln,
                                        bias=eps_col[:, 0:1], scale=1.0 / D), inc=True)
    Sc.wait(Sc, t_ln)
    t_rs = Sc.op(lambda s: s.activation(rstd_col, rstd_col, AF.Exp, scale=-0.5),
                 inc=True)
    return t_rs


def emit_l1_ffn(ctx, x_src, x_dst, W, scr, consts):
    T, V, Sc, G, Q = ctx.T, ctx.V, ctx.S, ctx.G, ctx.Q
    ident = consts["ident"]
    eps_col = consts["eps_col"]
    stL = ctx.scope()
    stU = contextlib.ExitStack()

    uT = ctx.sbuf("uT", [128, NCH * S], BF16, stU, side="right")
    stat_mv = ctx.sbuf("mv", [128, 2 * NT], F32, stL)
    stat_rs = ctx.sbuf("rs", [128, NT], F32, stL)
    gr_tm = ctx.sbuf("grtm", [128, NT * 8], F32, stL)
    anw = ctx.sbuf("anw", [128, D], F32, stL)
    fnw = ctx.sbuf("fnw", [128, D], F32, stL)
    mnw = ctx.sbuf("mnw", [128, D], F32, stL)
    rturw = ctx.sbuf("rtr", [128, NCH * 8], BF16, stL)
    o_dram = ctx.dram("o1_scr", [S, D], BF16)
    h_dram = ctx.dram("h_scr", [S, D], F32)
    xm_dram = ctx.dram("xm_scr", [S, D], F32)

    Q.op(lambda q: q.dma_start(anw[:, :], W["attn_norm_w_bcast"]), inc=True)
    Q.op(lambda q: q.dma_start(fnw[:, :], W["ff_norm_w_bcast"]), inc=True)
    Q.op(lambda q: q.dma_start(mnw[:, :], W["moe_norm_w_bcast"]), inc=True)
    t_rtr = G.op(lambda g: g.dma_start(
        rturw[:, :].rearrange("p (c e) -> p c e", c=NCH),
        W["L1_ffn_router_w"].rearrange("(c p) e -> p c e", p=128)), inc=True)

    sA = contextlib.ExitStack()
    p_r = [ctx.psum("pr", [128, 8], F32, sA) for _ in range(2)]
    sm = ctx.sbuf("smr", [128, 8], F32, sA)
    sm2 = ctx.sbuf("smr2", [128, 8], F32, sA)
    mxr = ctx.sbuf("mxr", [128, 1], F32, sA)
    rcr = ctx.sbuf("rcr", [128, 1], F32, sA)
    ssr = ctx.sbuf("ssr", [128, 1], F32, sA)
    state = {"t_scv": [0, 0]}

    def router_hook(t, t_ev):
        par = t % 2
        T.wait(V, t_ev)
        if t == 0:
            T.wait(G, t_rtr)
        T.wait(V, state["t_scv"][par])
        t_mm = 0
        for c in range(NCH):
            t_mm = T.op(lambda tt, c=c, t=t, par=par: tt.matmul(
                p_r[par][:, :], uT[:, c * S + t * 128:c * S + (t + 1) * 128],
                rturw[:, c * 8:(c + 1) * 8],
                start=(c == 0), stop=(c == NCH - 1)), inc=(c == NCH - 1))
        V.wait(T, t_mm)
        t_s = V.op(lambda v, par=par: v.tensor_copy(sm[:, :], p_r[par][:, :]),
                   inc=True)
        state["t_scv"][par] = t_s
        V.wait(V, t_s)
        t_m = V.op(lambda v: v.reduce_max(mxr[:, :], sm[:, :], axis=AX, negate=True),
                   inc=True)
        Sc.wait(V, t_m)
        t_e = Sc.op(lambda s: s.activation(sm[:, :], sm[:, :], AF.Exp, bias=mxr[:, :],
                                           scale=1.0, accum_out=ssr[:, :]), inc=True)
        V.wait(Sc, t_e)
        t_r = V.op(lambda v: v.reciprocal(rcr[:, :], ssr[:, :]), inc=True)
        V.wait(V, t_r)
        t_p = V.op(lambda v: v.tensor_scalar(sm[:, :], sm[:, :], rcr[:, :], None,
                                             op0=alu.mult), inc=True)
        V.wait(V, t_p)
        t_m1 = V.op(lambda v: v.reduce_max(mxr[:, :], sm[:, :], axis=AX), inc=True)
        V.wait(V, t_m1)
        t_ge = V.op(lambda v: v.tensor_scalar(sm2[:, :], sm[:, :], mxr[:, :], None,
                                              op0=alu.is_ge), inc=True)
        V.wait(V, t_ge)
        t_s2 = V.op(lambda v: v.scalar_tensor_tensor(sm2[:, :], sm2[:, :], -2.0,
                                                     sm[:, :], op0=alu.mult,
                                                     op1=alu.add), inc=True)
        V.wait(V, t_s2)
        t_m2 = V.op(lambda v: v.reduce_max(mxr[:, :], sm2[:, :], axis=AX), inc=True)
        V.wait(V, t_m2)
        t_k = V.op(lambda v: v.tensor_scalar(sm2[:, :], sm[:, :], mxr[:, :], None,
                                             op0=alu.is_ge), inc=True)
        V.wait(V, t_k)
        t_g = V.op(lambda v, t=t: v.tensor_tensor(
            gr_tm[:, t * 8:(t + 1) * 8], sm[:, :], sm2[:, :], op=alu.mult),
            inc=True)

    emit_ln_phaseA(ctx, x_src, uT, consts, sA, stats=(stat_mv, stat_rs),
                   hook=router_hook)
    ctx.flush("l1ffnA")
    sA.close()

    sB = contextlib.ExitStack()
    wsl3 = [[ctx.sbuf("wsl1", [128, NCH * 128], BF16, sB) for _ in range(3)]
            for _ in range(2)]
    qf = [ctx.sbuf("qf1", [64, S], BF16, sB) for _ in range(2)]
    kf = [ctx.sbuf("kf1", [64, S], BF16, sB) for _ in range(2)]
    v2e = ctx.sbuf("v2e", [128, NT * 132], BF16, sB)
    o_hgT = ctx.sbuf("ohgT", [128, S], BF16, sB)
    pbT = [ctx.sbuf("pbT", [128, 512], BF16, sB) for _ in range(4)]
    smd = [ctx.sbuf("smd1", [128, 512], F32, sB) for _ in range(2)]
    ob_sb = [ctx.sbuf("obsb", [64, 512], BF16, sB) for _ in range(2)]
    rcr_row = [ctx.sbuf("rcrr", [1, 512], F32, sB) for _ in range(2)]
    sq2 = ctx.sbuf("sq2", [64, S], BF16, sB)
    on64 = ctx.sbuf("on64", [64, 1], BF16, sB)
    mx8 = ctx.sbuf("mx8", [1, 8], F32, sB)
    mqk = ctx.sbuf("mqk", [1, 4], F32, sB)
    negBr = ctx.sbuf("negBr", [1, 2], F32, sB)
    negB = ctx.sbuf("negB", [128, 2], F32, sB)
    on1r = ctx.sbuf("on1r", [1, 128], F32, sB)
    p_qv = ctx.psum("pqv1", [128, 512], F32, sB)
    p_s = [ctx.psum("ps1", [128, 512], F32, sB) for _ in range(4)]
    p_oT = [ctx.psum("poT1", [65, 512], F32, sB) for _ in range(2)]
    p_rb = ctx.psum("prb1", [64, 512], F32, sB)
    mask_qb = ctx.sbuf("mqb", [128, 4 * 512], F32, sB)
    t_mtri = Q.op(lambda q: q.dma_start(
        mask_qb[:, :].rearrange("p (j m) -> p j m", j=4),
        W["mask_qb"].rearrange("j p m -> p j m")), inc=True)

    t_on = V.op(lambda v: v.memset(on64[:, :], 1.0), inc=True)
    V.op(lambda v: v.memset(on1r[:, :], 1.0))
    V.op(lambda v: v.memset(v2e[:, :], 1.0))
    t_v2e1 = V.cnt + 0
    T.wait(V, t_on)

    t_hgT = 0
    t_pq_ev = 0
    t_pv_ev = 0
    t_nmL = [0]
    t_nbG = 0
    t_nbS = [0, 0]
    t_psfree = [("S", 0)] * 4
    t_avm = [0, 0, 0, 0]
    t_sq = [0]
    t_nml = [0]
    t_vmax = [0]
    t_nbL = [0]
    pend_nm = []
    t_smdS = [0, 0]
    t_oevS = [0, 0]
    t_rrV = [0, 0]
    t_rbT = [0, 0]
    t_obV = [0, 0]
    t_rbmV = [0]
    t_oevV = [0]
    t_avlast = [0]
    bs_rot = [0]
    pnorm = [None]
    t_nev = 0

    V.wait(Q, t_mtri)
    for hg in range(8):
        par = hg % 2
        G.wait(T, t_hgT)
        for r, wname in enumerate(("L1_ffn_wq", "L1_ffn_wk", "L1_ffn_wv")):
            src = ap3(W[wname], hg * 128, 128, 1024, NCH, 128 * 1024, 128)
            t_w = G.op(lambda g, r=r, par=par, src=src: g.dma_start(
                wsl3[par][r][:, :].rearrange("p (k m) -> p k m", k=NCH), src),
                inc=True)
        T.wait(G, t_w)

        def rot_sp():
            sp = bs_rot[0] % 4
            bs_rot[0] += 1
            return sp

        def wait_free(sp):
            feng, ftok = t_psfree[sp]
            if feng == "S":
                T.wait(Sc, ftok)
            else:
                T.wait(V, ftok)

        for si, (which, hoff) in enumerate(((0, 0), (0, 1), (1, 0), (1, 1))):
            dstf = qf[hoff] if which == 0 else kf[hoff]
            sc_ = 0.125 if which == 0 else 1.0
            for blk in range(S // 512):
                sp = rot_sp()
                wait_free(sp)
                t_mm = 0
                for k in range(NCH):
                    t_mm = T.op(lambda tt, k=k, blk=blk, which=which, hoff=hoff,
                                par=par, sp=sp: tt.matmul(
                        p_s[sp][0:64, :],
                        wsl3[par][which][:, k * 128 + hoff * 64:
                                         k * 128 + hoff * 64 + 64],
                        uT[:, k * S + blk * 512:k * S + (blk + 1) * 512],
                        start=(k == 0), stop=(k == NCH - 1)), inc=(k == NCH - 1))
                Sc.wait(T, t_mm)
                t_pq_ev = Sc.op(lambda s, blk=blk, sc_=sc_, dstf=dstf, sp=sp:
                                s.activation(dstf[0:64, blk * 512:(blk + 1) * 512],
                                             p_s[sp][0:64, :], AF.Copy, bias=0.0,
                                             scale=sc_), inc=True)
                t_psfree[sp] = ("S", t_pq_ev)
                if pend_nm:
                    pend_nm.pop(0)()
            while pend_nm:
                pend_nm.pop(0)()
            Sc.wait(T, t_nml[0])
            Sc.wait(Sc, t_pq_ev)
            t_sq[0] = Sc.op(lambda s, dstf=dstf: s.activation(
                sq2[:, :], dstf[0:64, :], AF.Square), inc=True)

            def mk_nm(b, t_sq_=t_sq[0]):
                def go():
                    sp = rot_sp()
                    wait_free(sp)
                    T.wait(Sc, t_sq_)
                    t_nm = T.op(lambda tt, b=b, sp=sp: tt.matmul(
                        p_s[sp][0:1, :], on64[:, :],
                        sq2[:, b * 512:(b + 1) * 512],
                        start=True, stop=True), inc=True)
                    t_nml[0] = t_nm
                    V.wait(T, t_nm)
                    V.wait(V, t_vmax[0])
                    t_rm = V.op(lambda v, b=b, sp=sp: v.reduce_max(
                        mx8[0:1, b:b + 1], p_s[sp][0:1, :], axis=AX), inc=True)
                    t_vmax[0] = t_rm
                    t_psfree[sp] = ("V", t_rm)
                return go

            def mk_fin(si=si, which=which, hoff=hoff):
                def go():
                    V.wait(V, t_vmax[0])
                    t_fm = V.op(lambda v, si=si: v.reduce_max(
                        mqk[0:1, si:si + 1], mx8[0:1, 0:S // 512], axis=AX),
                        inc=True)
                    t_vmax[0] = t_fm
                    if which == 1:
                        h = hoff
                        V.wait(V, t_fm)
                        V.wait(Sc, t_nbS[h])
                        t_b1 = V.op(lambda v, h=h: v.tensor_scalar(
                            negBr[:, h:h + 1], mqk[0:1, h:h + 1],
                            mqk[0:1, 2 + h:3 + h], None, op0=alu.mult), inc=True)
                        Sc.wait(V, t_b1)
                        t_b2 = Sc.op(lambda s, h=h: s.activation(
                            negBr[:, h:h + 1], negBr[:, h:h + 1], AF.Ln,
                            bias=consts["epsb"][0:1, 0:1], scale=1.0), inc=True)
                        Sc.wait(Sc, t_b2)
                        t_b3 = Sc.op(lambda s, h=h: s.activation(
                            negBr[:, h:h + 1], negBr[:, h:h + 1], AF.Exp,
                            scale=0.5), inc=True)
                        V.wait(Sc, t_b3)
                        t_nbL[0] = V.op(lambda v, h=h: v.tensor_scalar(
                            negBr[:, h:h + 1], negBr[:, h:h + 1], -1.0, None,
                            op0=alu.mult), inc=True)
                return go

            pend_nm.extend(mk_nm(b) for b in range(S // 512))
            pend_nm.append(mk_fin())
        t_qk = t_pq_ev
        V.wait(T, t_hgT)
        if hg == 0:
            V.wait(V, t_v2e1)
        for t in range(NT):
            sp = rot_sp()
            wait_free(sp)
            t_mm = 0
            for k in range(NCH):
                t_mm = T.op(lambda tt, k=k, t=t, par=par, sp=sp: tt.matmul(
                    p_s[sp][:, 0:128],
                    uT[:, k * S + t * 128:k * S + (t + 1) * 128],
                    wsl3[par][2][:, k * 128:(k + 1) * 128],
                    start=(k == 0), stop=(k == NCH - 1)), inc=(k == NCH - 1))
            V.wait(T, t_mm)
            V.op(lambda v, t=t, sp=sp: v.tensor_copy(
                v2e[:, t * 132:t * 132 + 64], p_s[sp][:, 0:64]))
            t_pv_ev = V.op(lambda v, t=t, sp=sp: v.tensor_copy(
                v2e[:, t * 132 + 65:t * 132 + 129], p_s[sp][:, 64:128]), inc=True)
            t_psfree[sp] = ("V", t_pv_ev)
            if pend_nm:
                pend_nm.pop(0)()
        while pend_nm:
            pend_nm.pop(0)()
        T.wait(V, t_nbL[0])
        T.wait(V, t_nbG)
        t_nbm = T.op(lambda tt: tt.matmul(p_qv[:, 0:2], on1r[:, :], negBr[:, :],
                                          start=True, stop=True), inc=True)
        V.wait(T, t_nbm)
        V.wait(Sc, t_nbS[0])
        V.wait(Sc, t_nbS[1])
        t_nbG = V.op(lambda v: v.tensor_copy(negB[:, :], p_qv[:, 0:2]), inc=True)
        Sc.wait(V, t_nbG)
        T.wait(Sc, t_qk)
        def emit_av(pe, h, qb, nk, poT, it_p):
            pkt, psp, pt_e = pe
            T.wait(Sc, pt_e)
            if pkt == 0:
                T.wait(Sc, t_oevS[it_p])
                T.wait(V, t_rrV[it_p])
            t_av = T.op(lambda tt, h=h, kt=pkt, sp=psp, nk=nk, poT=poT:
                        tt.matmul(
                poT[:, :],
                v2e[:, kt * 132 + h * 65:kt * 132 + h * 65 + 65],
                pbT[sp][:, :],
                start=(kt == 0), stop=(kt == nk - 1)), inc=True)
            t_avm[psp] = t_av
            t_avlast[0] = t_av

        for h in range(2):
            bias_ap = negB[:, h:h + 1]
            for qb in range(S // 512):
                it_p = (h * (S // 512) + qb) % 2
                nk = qb * 4 + 4
                poT = p_oT[it_p]
                pends = []
                ndiag = 0
                npop = 0
                for kt in range(nk):
                    sp = bs_rot[0] % 4
                    bs_rot[0] += 1
                    diag = kt >= qb * 4
                    feng, ftok = t_psfree[sp]
                    if feng == "S":
                        T.wait(Sc, ftok)
                    else:
                        T.wait(V, ftok)
                    if kt == 0 and qb == 0 and h == 0:
                        T.wait(V, t_pv_ev)
                    t_mm = T.op(lambda tt, h=h, qb=qb, kt=kt, sp=sp: tt.matmul(
                        p_s[sp][:, :],
                        kf[h][0:64, kt * 128:(kt + 1) * 128],
                        qf[h][0:64, qb * 512:(qb + 1) * 512],
                        start=True, stop=True), inc=True)
                    if diag:
                        dsp = ndiag % 2
                        ndiag += 1
                        j = kt - qb * 4
                        V.wait(T, t_mm)
                        V.wait(Sc, t_smdS[dsp])
                        t_md = V.op(lambda v, sp=sp, dsp=dsp, j=j: v.tensor_tensor(
                            smd[dsp][:, :], p_s[sp][:, :],
                            mask_qb[:, j * 512:(j + 1) * 512], op=alu.add), inc=True)
                        t_psfree[sp] = ("V", t_md)
                        Sc.wait(V, t_md)
                        Sc.wait(T, t_avm[sp])
                        t_e = Sc.op(lambda s, sp=sp, dsp=dsp, bias_ap=bias_ap:
                                    s.activation(
                            pbT[sp][:, :], smd[dsp][:, :], AF.Exp,
                            bias=bias_ap, scale=1.0), inc=True)
                        t_smdS[dsp] = t_e
                    else:
                        Sc.wait(T, t_mm)
                        Sc.wait(T, t_avm[sp])
                        t_e = Sc.op(lambda s, sp=sp, bias_ap=bias_ap:
                                    s.activation(
                            pbT[sp][:, :], p_s[sp][:, :],
                            AF.Exp, bias=bias_ap, scale=1.0), inc=True)
                        t_psfree[sp] = ("S", t_e)
                    t_nbS[h] = t_e
                    pends.append((kt, sp, t_e))
                    if len(pends) > 3:
                        emit_av(pends.pop(0), h, qb, nk, poT, it_p)
                        npop += 1
                        if npop == 2 and pnorm[0] is not None:
                            pnorm[0]()
                            pnorm[0] = None
                for di, pe_ in enumerate(pends):
                    emit_av(pe_, h, qb, nk, poT, it_p)
                    if pnorm[0] is not None and npop + di >= 1:
                        pnorm[0]()
                        pnorm[0] = None
                t_av = t_avlast[0]
                V.wait(T, t_av)
                V.wait(T, t_rbT[it_p])
                t_rr = V.op(lambda v, it_p=it_p, poT=poT: v.reciprocal(
                    rcr_row[it_p][:, :], poT[64:65, :]), inc=True)
                t_rrV[it_p] = t_rr
                Sc.wait(T, t_av)
                Sc.wait(V, t_obV[it_p])
                t_ob = Sc.op(lambda s, it_p=it_p, poT=poT: s.copy(
                    ob_sb[it_p][:, :], poT[0:64, :]), inc=True)
                t_oevS[it_p] = t_ob

                def mk_norm(h=h, qb=qb, it_p=it_p, t_rr=t_rr, t_ob=t_ob):
                    def go():
                        T.wait(V, t_rr)
                        T.wait(V, t_rbmV[0])
                        t_rb = T.op(lambda tt, it_p=it_p: tt.matmul(
                            p_rb[:, :], on1r[0:1, 0:64], rcr_row[it_p][:, :],
                            start=True, stop=True), inc=True)
                        t_rbT[it_p] = t_rb
                        V.wait(T, t_rb)
                        V.wait(Sc, t_ob)
                        t_mu = V.op(lambda v, h=h, qb=qb, it_p=it_p:
                                    v.tensor_tensor(
                            o_hgT[h * 64:h * 64 + 64, qb * 512:(qb + 1) * 512],
                            ob_sb[it_p][:, :], p_rb[0:64, :], op=alu.mult),
                            inc=True)
                        t_rbmV[0] = t_mu
                        t_obV[it_p] = t_mu
                        t_oevV[0] = t_mu
                    return go
                pnorm[0] = mk_norm()
        if pnorm[0] is not None:
            pnorm[0]()
            pnorm[0] = None
        t_hgT = t_avlast[0]
        Q.wait(ctx.V, t_oevV[0])
        t_odma = Q.op(lambda q, hg=hg: q.dma_start(
            ap2(o_dram, hg * 128 * S, 128, S, S), o_hgT[:, :]), inc=True)
        V.wait(Q, t_odma)
    ctx.flush("l1ffnB")
    sB.close()

    sE = contextlib.ExitStack()
    ew_s = [ctx.sbuf("ews", [128, 3 * NCH * 1024], BF16, sE) for _ in range(2)]
    geb = [ctx.sbuf("geb", [128, NCH * 512], BF16, sE) for _ in range(2)]
    acc = [ctx.sbuf("macc", [128, D], F32, sE) for _ in range(4)]
    sqs2 = ctx.sbuf("sqs2", [128, D], BF16, sE)
    src2 = ctx.sbuf("src2", [128, 2], F32, sE)
    ssc2 = src2[:, 0:1]
    rsc2_col = src2[:, 1:2]
    p_ab = [ctx.psum("pab", [128, 512], F32, sE) for _ in range(4)]
    p_eo = [ctx.psum("peo", [128, 1024], F32, sE) for _ in range(2)]
    NEX1 = 8

    t_ewQ = [0, 0]
    t_eo_last = [0]
    t_mmE = [0, 0]
    t_geV = [0, 0]
    t_sil = [0, 0]
    t_eoV = [0, 0]
    t_accV = [[0] * 4, [0] * 4]
    t_acc = [0, 0, 0, 0]
    t_xmq = [0, 0, 0, 0]
    t_rscV = 0

    for b in range(NB):
        for e in range(NEX1):
            par = e % 2
            Q.wait(T, t_mmE[par])
            src_ = bass.AP(scr["ewall"].tensor,
                           scr["ewall"].offset + e * 3 * 1024 * 1024,
                           [[1024, 128], [1024 * 1024, 3], [128 * 1024, 8],
                            [1, 1024]])
            t_we = Q.op(lambda q, par=par, src_=src_: q.dma_start(
                ew_s[par][:, :].rearrange("p (c k m) -> p c k m", c=3, k=NCH),
                src_), inc=True)
            t_ewQ[par] = t_we
            T.wait(Q, t_we)
            for hc in range(NCH):
                sp = hc % 2
                T.wait(Sc, t_sil[sp])
                T.wait(V, t_geV[sp])
                t_mma = 0
                for k in range(NCH):
                    t_mma = T.op(lambda tt, k=k, hc=hc, par=par, b=b, sp=sp: tt.matmul(
                        p_ab[sp][:, :],
                        ew_s[par][:, k * 1024 + hc * 128:k * 1024 + (hc + 1) * 128],
                        uT[:, k * S + b * 512:k * S + (b + 1) * 512],
                        start=(k == 0), stop=(k == NCH - 1)), inc=(k == NCH - 1))
                t_mmb = 0
                for k in range(NCH):
                    t_mmb = T.op(lambda tt, k=k, hc=hc, par=par, b=b, sp=sp: tt.matmul(
                        p_ab[2 + sp][:, :],
                        ew_s[par][:, 8192 + k * 1024 + hc * 128:
                                  8192 + k * 1024 + (hc + 1) * 128],
                        uT[:, k * S + b * 512:k * S + (b + 1) * 512],
                        start=(k == 0), stop=(k == NCH - 1)), inc=(k == NCH - 1))
                Sc.wait(T, t_mma)
                Sc.wait(V, t_geV[sp])
                t_si = Sc.op(lambda s, hc=hc, sp=sp, par=par: s.activation(
                    geb[par][:, hc * 512:(hc + 1) * 512], p_ab[sp][:, :],
                    AF.Silu), inc=True)
                t_sil[sp] = t_si
                V.wait(Sc, t_si)
                V.wait(T, t_mmb)
                t_ge_ = V.op(lambda v, hc=hc, sp=sp, par=par: v.tensor_tensor(
                    geb[par][:, hc * 512:(hc + 1) * 512],
                    geb[par][:, hc * 512:(hc + 1) * 512],
                    p_ab[2 + sp][:, :], op=alu.mult), inc=True)
                t_geV[sp] = t_ge_
            T.wait(V, t_geV[0])
            T.wait(V, t_geV[1])
            for t4 in range(4):
                ti = b * 4 + t4
                ep = t4 % 2
                T.wait(V, t_eoV[ep])
                t_mm = 0
                for hh in range(2):
                    for k in range(NCH):
                        t_mm = T.op(lambda tt, t4=t4, hh=hh, k=k, par=par, ep=ep:
                                    tt.matmul(
                            p_eo[ep][:, hh * 512:(hh + 1) * 512],
                            geb[par][:, k * 512 + t4 * 128:k * 512 + (t4 + 1) * 128],
                            ew_s[par][:, 16384 + k * 1024 + hh * 512:
                                      16384 + k * 1024 + (hh + 1) * 512],
                            start=(k == 0), stop=(k == NCH - 1)),
                            inc=(hh == 1 and k == NCH - 1))
                t_mmE[par] = t_mm
                t_eo_last[0] = t_mm
                V.wait(T, t_mm)
                V.wait(V, t_acc[t4])
                if e == 0:
                    V.wait(Q, t_xmq[t4])
                    t_a = V.op(lambda v, t4=t4, ti=ti: v.tensor_scalar(
                        acc[t4][:, :], p_eo[t4 % 2][:, :],
                        gr_tm[:, ti * 8 + 0:ti * 8 + 1], None, op0=alu.mult),
                        inc=True)
                else:
                    t_a = V.op(lambda v, t4=t4, ti=ti, e=e: v.scalar_tensor_tensor(
                        acc[t4][:, :], p_eo[t4 % 2][:, :],
                        gr_tm[:, ti * 8 + e:ti * 8 + e + 1], acc[t4][:, :],
                        op0=alu.mult, op1=alu.add), inc=True)
                t_acc[t4] = t_a
                t_eoV[ep] = t_a
        for t4 in range(4):
            ti = b * 4 + t4
            Sc.wait(V, t_acc[t4])
            Sc.wait(V, t_rscV)
            t_rs3 = rms_cols(ctx, acc[t4], sqs2, ssc2, rsc2_col, eps_col)
            V.wait(Sc, t_rs3)
            V.wait(V, t_acc[t4])
            t_x1 = V.op(lambda v, t4=t4: v.tensor_scalar(
                acc[t4][:, :], acc[t4][:, :], rsc2_col, 0.5,
                op0=alu.mult, op1=alu.mult), inc=True)
            t_rscV = t_x1
            V.wait(V, t_x1)
            t_x2 = V.op(lambda v, t4=t4: v.tensor_tensor(
                acc[t4][:, :], acc[t4][:, :], mnw[:, :], op=alu.mult), inc=True)
            t_acc[t4] = t_x2
            Q.wait(V, t_x2)
            t_xmq[t4] = Q.op(lambda q, ti=ti, t4=t4: q.dma_start(
                xm_dram[ti * 128:(ti + 1) * 128, :], acc[t4][:, :]), inc=True)
    ctx.flush("l1ffnE")
    sE.close()
    stU.close()

    sC = contextlib.ExitStack()
    hT = ctx.sbuf("hT", [128, NCH * S], BF16, stL)
    wo_sb = ctx.sbuf("wo1", [128, NCH * D], BF16, sC)
    ot = [ctx.sbuf("ot1", [128, D], BF16, sC) for _ in range(2)]
    xb = [ctx.sbuf("xb1", [128, D], F32, sC) for _ in range(2)]
    ao = [ctx.sbuf("ao1", [128, D], F32, sC) for _ in range(2)]
    hb = [ctx.sbuf("hb1", [128, D], F32, sC) for _ in range(2)]
    hbb = [ctx.sbuf("hbb1", [128, D], BF16, sC) for _ in range(2)]
    sqs = ctx.sbuf("sqs", [128, D], BF16, sC)
    ssc = ctx.sbuf("ssc", [128, 1], F32, sC)
    rsc = ctx.sbuf("rsc", [128, 1], F32, sC)
    p_tr2 = [ctx.psum("ptr21", [128, 1024], BF16, sC) for _ in range(2)]
    p_wo = [ctx.psum("pwo", [128, 1024], F32, sC) for _ in range(2)]
    t_wo = G.op(lambda g: g.dma_start(
        wo_sb[:, :].rearrange("p (k m) -> p k m", k=NCH),
        ap3(W["L1_ffn_wo"], 0, 128, 1024, NCH, 128 * 1024, 1024)), inc=True)
    t_trc = [0, 0]
    t_otv = [0, 0]
    t_otm = [0, 0]
    t_wov = [0, 0]
    t_hq = [0, 0]
    t_hTev = [0, 0]
    t_xr = [0, 0]
    for t in range(NT):
        par = t % 2
        Q.wait(V, t_xr[par])
        t_x = Q.op(lambda q, t=t, par=par: q.dma_start(
            xb[par][:, :], x_src[t * 128:(t + 1) * 128, :]), inc=True)
        Q.wait(T, t_otm[par])
        t_ol = Q.op(lambda q, t=t, par=par: q.dma_start(
            ot[par][:, :].rearrange("p (c m) -> p c m", c=NCH),
            ap3(o_dram, t * 128, 128, S, NCH, 128 * S, 128)), inc=True)
        if t == 0:
            T.wait(G, t_wo)
        T.wait(Q, t_ol)
        T.wait(V, t_wov[par])
        t_mm = 0
        for k in range(NCH):
            T.op(lambda tt, k=k, par=par: tt.matmul(
                p_wo[par][:, 0:512], ot[par][:, k * 128:(k + 1) * 128],
                wo_sb[:, k * 1024:k * 1024 + 512],
                start=(k == 0), stop=(k == NCH - 1)))
            t_mm = T.op(lambda tt, k=k, par=par: tt.matmul(
                p_wo[par][:, 512:1024], ot[par][:, k * 128:(k + 1) * 128],
                wo_sb[:, k * 1024 + 512:(k + 1) * 1024],
                start=(k == 0), stop=(k == NCH - 1)), inc=(k == NCH - 1))
        t_otm[par] = t_mm
        V.wait(T, t_mm)
        t_ao = V.op(lambda v, par=par: v.tensor_copy(ao[par][:, :], p_wo[par][:, :]),
                    inc=True)
        t_wov[par] = t_ao
        Sc.wait(V, t_ao)
        t_rs2 = rms_cols(ctx, ao[par], sqs, ssc[:, 0:1], rsc[:, 0:1], eps_col)
        V.wait(Sc, t_rs2)
        V.wait(V, t_ao)
        t_n1 = V.op(lambda v, par=par: v.tensor_scalar(
            ao[par][:, :], ao[par][:, :], rsc[:, 0:1], None, op0=alu.mult), inc=True)
        V.wait(V, t_n1)
        t_n2 = V.op(lambda v, par=par: v.tensor_tensor(
            ao[par][:, :], ao[par][:, :], anw[:, :], op=alu.mult), inc=True)
        V.wait(Q, t_x)
        V.wait(Q, t_hq[par])
        t_u = V.op(lambda v, t=t, par=par: v.tensor_scalar(
            hb[par][:, :], xb[par][:, :], stat_mv[:, 2 * t:2 * t + 1],
            stat_rs[:, t:t + 1], op0=alu.subtract, op1=alu.mult), inc=True)
        t_xr[par] = t_u
        V.wait(V, t_u)
        t_h = V.op(lambda v, par=par: v.tensor_tensor(
            hb[par][:, :], hb[par][:, :], ao[par][:, :], op=alu.add), inc=True)
        Q.wait(V, t_h)
        t_hq[par] = Q.op(lambda q, t=t, par=par: q.dma_start(
            h_dram[t * 128:(t + 1) * 128, :], hb[par][:, :]), inc=True)
        Sc.wait(V, t_h)
        Sc.wait(T, t_trc[par])
        t_hbb = Sc.op(lambda s, par=par: s.copy(hbb[par][:, :], hb[par][:, :]),
                      inc=True)
        T.wait(Sc, t_hbb)
        T.wait(V, t_hTev[par])
        t_tr2 = 0
        for c in range(NCH):
            t_tr2 = T.op(lambda tt, c=c, par=par: tt.transpose(
                p_tr2[par][:, c * 128:(c + 1) * 128],
                hbb[par][:, c * 128:(c + 1) * 128], ident[:, :]),
                inc=(c == NCH - 1))
        t_trc[par] = t_tr2
        V.wait(T, t_tr2)
        t_hTe = V.op(lambda v, t=t, par=par: v.tensor_copy(
            hT[:, :].rearrange("p (c s) -> p c s", c=NCH)[:, :, t * 128:(t + 1) * 128],
            p_tr2[par][:, :].rearrange("p (c w) -> p c w", c=NCH)), inc=True)
        t_hTev[par] = t_hTe
    ctx.flush("l1ffnC")
    sC.close()

    sD = contextlib.ExitStack()
    w1b = [ctx.sbuf("w1b", [128, NCH * 512], BF16, sD) for _ in range(2)]
    w3b = [ctx.sbuf("w3b", [128, NCH * 512], BF16, sD) for _ in range(2)]
    w2b = [ctx.sbuf("w2b", [128, 4 * 1024], BF16, sD) for _ in range(2)]
    mid = ctx.sbuf("mid", [128, 32 * 512], BF16, sD)
    sa2 = [ctx.sbuf("sa2", [128, 512], BF16, sD) for _ in range(2)]
    fo = [ctx.sbuf("fo", [128, D], F32, sD) for _ in range(2)]
    xl = [ctx.sbuf("xl", [128, D], F32, sD) for _ in range(2)]
    hl = [ctx.sbuf("hl", [128, D], F32, sD) for _ in range(2)]
    xml = [ctx.sbuf("xml", [128, D], F32, sD) for _ in range(2)]
    sqs3 = ctx.sbuf("sqs3", [128, D], BF16, sD)
    ssc3 = ctx.sbuf("ssc3", [128, 1], F32, sD)
    rsc3 = ctx.sbuf("rsc3", [128, 1], F32, sD)
    p_ab2 = [ctx.psum("pab2", [128, 512], F32, sD) for _ in range(4)]
    p_f = [ctx.psum("pf", [128, 1024], F32, sD) for _ in range(2)]

    t_w1q = [0, 0]
    t_w3g = [0, 0]
    t_w2g = [0, 0]
    t_silD = [0, 0]
    t_midV = [0, 0]
    t_m13 = [0, 0]
    t_m2 = [0, 0]
    t_fv = [0, 0]
    t_xq = [0, 0]
    t_stD = [0, 0]
    t_rscV3 = 0
    for b in range(NB):
        for m in range(32):
            sp = m % 2
            gp = (m // 4) % 2
            if m == 0:
                V.wait(T, t_m2[0])
                V.wait(T, t_m2[1])
            if m % 4 == 0:
                Q.wait(T, t_m13[gp])
                t_w1q[gp] = Q.op(lambda q, m=m, gp=gp: q.dma_start(
                    w1b[gp][:, :].rearrange("p (k m2) -> p k m2", k=NCH),
                    ap3(scr["w1"], m * 128, 128, 4096, NCH, 128 * 4096, 512)),
                    inc=True)
                t_w3g[gp] = Q.op(lambda q, m=m, gp=gp: q.dma_start(
                    w3b[gp][:, :].rearrange("p (k m2) -> p k m2", k=NCH),
                    ap3(scr["w3"], m * 128, 128, 4096, NCH, 128 * 4096, 512)),
                    inc=True)
                T.wait(Q, t_w1q[gp])
                T.wait(Q, t_w3g[gp])
            mo = (m % 4) * 128
            T.wait(Sc, t_silD[sp])
            T.wait(V, t_midV[sp])
            t_mma = 0
            for k in range(NCH):
                t_mma = T.op(lambda tt, k=k, b=b, sp=sp, gp=gp, mo=mo: tt.matmul(
                    p_ab2[sp][:, :],
                    w1b[gp][:, k * 512 + mo:k * 512 + mo + 128],
                    hT[:, k * S + b * 512:k * S + (b + 1) * 512],
                    start=(k == 0), stop=(k == NCH - 1)), inc=(k == NCH - 1))
            t_mmb = 0
            for k in range(NCH):
                t_mmb = T.op(lambda tt, k=k, b=b, sp=sp, gp=gp, mo=mo: tt.matmul(
                    p_ab2[2 + sp][:, :],
                    w3b[gp][:, k * 512 + mo:k * 512 + mo + 128],
                    hT[:, k * S + b * 512:k * S + (b + 1) * 512],
                    start=(k == 0), stop=(k == NCH - 1)), inc=(k == NCH - 1))
            if m % 4 == 3:
                t_m13[gp] = t_mmb
            Sc.wait(T, t_mma)
            t_si = Sc.op(lambda s, sp=sp: s.activation(
                sa2[sp][:, :], p_ab2[sp][:, :], AF.Silu), inc=True)
            t_silD[sp] = t_si
            V.wait(Sc, t_si)
            V.wait(T, t_mmb)
            t_mid = V.op(lambda v, m=m, sp=sp: v.tensor_tensor(
                mid[:, m * 512:(m + 1) * 512], sa2[sp][:, :], p_ab2[2 + sp][:, :],
                op=alu.mult), inc=True)
            t_midV[sp] = t_mid
        for tp in range(2):
            for kg in range(8):
                sp = kg % 2
                G.wait(T, t_m2[sp])
                t_w2g[sp] = G.op(lambda g, kg=kg, sp=sp: g.dma_start(
                    w2b[sp][:, :].rearrange("p (k m2) -> p k m2", k=4),
                    ap3(scr["w2"], kg * 4 * 128 * 1024, 128, 1024, 4,
                        128 * 1024, 1024)), inc=True)
                T.wait(G, t_w2g[sp])
                if kg == 0:
                    T.wait(V, t_midV[0])
                    T.wait(V, t_midV[1])
                    T.wait(V, t_fv[0])
                    T.wait(V, t_fv[1])
                t_mm = 0
                for kk in range(4):
                    k = kg * 4 + kk
                    for t2 in range(2):
                        t4 = tp * 2 + t2
                        for hh in range(2):
                            t_mm = T.op(lambda tt, t4=t4, t2=t2, hh=hh, k=k, kk=kk,
                                        sp=sp: tt.matmul(
                                p_f[t2][:, hh * 512:(hh + 1) * 512],
                                mid[:, k * 512 + t4 * 128:k * 512 + (t4 + 1) * 128],
                                w2b[sp][:, kk * 1024 + hh * 512:
                                        kk * 1024 + (hh + 1) * 512],
                                start=(k == 0), stop=(k == 31)),
                                inc=(t2 == 1 and hh == 1 and kk == 3))
                t_m2[sp] = t_mm
            t_mmD = t_mm
            for t2 in range(2):
                t4 = tp * 2 + t2
                ti = b * 4 + t4
                par = t2
                Q.wait(V, t_xq[par])
                t_lx = Q.op(lambda q, ti=ti, par=par: q.dma_start(
                    xl[par][:, :], x_src[ti * 128:(ti + 1) * 128, :]), inc=True)
                t_lh = Q.op(lambda q, ti=ti, par=par: q.dma_start(
                    hl[par][:, :], h_dram[ti * 128:(ti + 1) * 128, :]), inc=True)
                t_lm = Q.op(lambda q, ti=ti, par=par: q.dma_start(
                    xml[par][:, :], xm_dram[ti * 128:(ti + 1) * 128, :]), inc=True)
                V.wait(T, t_mmD)
                t_f = V.op(lambda v, t2=t2, par=par: v.tensor_copy(
                    fo[par][:, :], p_f[t2][:, :]), inc=True)
                t_fv[t2] = t_f
                Sc.wait(V, t_f)
                Sc.wait(V, t_rscV3)
                t_rs4 = rms_cols(ctx, fo[par], sqs3, ssc3[:, 0:1], rsc3[:, 0:1],
                                 eps_col)
                V.wait(Sc, t_rs4)
                V.wait(V, t_f)
                t_n1 = V.op(lambda v, par=par: v.tensor_scalar(
                    fo[par][:, :], fo[par][:, :], rsc3[:, 0:1], None, op0=alu.mult),
                    inc=True)
                t_rscV3 = t_n1
                V.wait(V, t_n1)
                t_n2 = V.op(lambda v, par=par: v.tensor_tensor(
                    fo[par][:, :], fo[par][:, :], fnw[:, :], op=alu.mult), inc=True)
                V.wait(Q, t_lm)
                V.wait(Q, t_stD[par])
                V.wait(V, t_n2)
                t_c1 = V.op(lambda v, par=par: v.tensor_tensor(
                    fo[par][:, :], fo[par][:, :], hl[par][:, :], op=alu.add), inc=True)
                V.wait(V, t_c1)
                t_c2 = V.op(lambda v, par=par: v.scalar_tensor_tensor(
                    fo[par][:, :], fo[par][:, :], 0.5, xl[par][:, :],
                    op0=alu.mult, op1=alu.add), inc=True)
                V.wait(V, t_c2)
                t_c3 = V.op(lambda v, par=par: v.tensor_tensor(
                    fo[par][:, :], fo[par][:, :], xml[par][:, :], op=alu.add),
                    inc=True)
                t_xq[par] = t_c3
                Q.wait(V, t_c3)
                t_stD[par] = Q.op(lambda q, ti=ti, par=par: q.dma_start(
                    x_dst[ti * 128:(ti + 1) * 128, :], fo[par][:, :]), inc=True)
    ctx.flush("l1ffnD")
    sD.close()
    stL.close()


def build_l1_ffn_program():
    nc = bass.Bass("TRN2", target_bir_lowering=False)
    stack = contextlib.ExitStack()
    x_in = nc.dram_tensor("x_in", [S, D], F32, kind="ExternalInput").ap()
    x_out = nc.dram_tensor("x_out", [S, D], F32, kind="ExternalOutput").ap()
    aps = {}
    for k, shp in (("wq", [1024, 1024]), ("wk", [1024, 1024]), ("wv", [1024, 1024]),
                   ("wo", [1024, 1024]), ("w1", [1024, 4096]), ("w2", [4096, 1024]),
                   ("w3", [1024, 4096]), ("router_w", [1024, 8])):
        aps["L1_ffn_" + k] = nc.dram_tensor("L1_ffn_" + k, shp, F32,
                                            kind="ExternalInput").ap()
    aps["ew_all"] = nc.dram_tensor("ew_all", [8, 3, 1024, 1024], F32,
                                   kind="ExternalInput").ap()
    for k in ("attn_norm_w_bcast", "ff_norm_w_bcast", "moe_norm_w_bcast"):
        aps[k] = nc.dram_tensor(k, [128, 1024], F32, kind="ExternalInput").ap()
    aps["mask_tri"] = nc.dram_tensor("mask_tri", [128, 128], F32,
                                     kind="ExternalInput").ap()
    aps["mask_triT"] = nc.dram_tensor("mask_triT", [128, 128], F32,
                                      kind="ExternalInput").ap()
    aps["mask_qb"] = nc.dram_tensor("mask_qb", [4, 128, 512], F32,
                                    kind="ExternalInput").ap()
    aps["c_ident"] = nc.dram_tensor("c_ident", [128, 128], BF16,
                                    kind="ExternalInput").ap()
    with stack:
        ctx = Ctx(nc, stack)
        consts = make_consts(ctx, aps)
        scr = emit_weightprep(ctx, aps, {"ewall", "w123"})
        for dst_, src_ in scr["bg"]:
            ctx.G.op(lambda g, dst_=dst_, src_=src_: g.dma_start(dst_, src_),
                     inc=True)
        ctx.flush("wprepL1")
        emit_l1_ffn(ctx, x_in, x_out, aps, scr, consts)
        ctx.flush("fin")
    return nc



def stage_inputs(inputs):
    W = {}
    W["L0_ffn_WrT"] = np.ascontiguousarray(inputs["L0_ffn_Wr"].transpose(0, 2, 1))
    W["L0_ffn_WsT"] = np.ascontiguousarray(inputs["L0_ffn_Ws"].transpose(0, 2, 1))
    W["L0_ffn_centT"] = np.ascontiguousarray(inputs["L0_ffn_centroids"].T)
    W["L0_ffn_br"] = inputs["L0_ffn_br"]
    W["ebias_bcast"] = np.ascontiguousarray(
        np.broadcast_to(inputs["L0_ffn_expert_biases"][None, :], (128, 16)))
    W["bs0_bcast"] = np.ascontiguousarray(
        np.broadcast_to(inputs["L0_ffn_bs"][0][None, :], (128, D)))
    W["bs1_bcast"] = np.ascontiguousarray(
        np.broadcast_to(inputs["L0_ffn_bs"][1][None, :], (128, D)))
    W["L0_attn_W_qkv"] = inputs["L0_attn_W_qkv"]
    W["L0_attn_W_out"] = inputs["L0_attn_W_out"]
    for k in ("W_i", "W_f", "W_g", "W_o"):
        W["L1_attn_" + k + "T"] = np.ascontiguousarray(inputs["L1_attn_" + k].T)
    for k in ("wq", "wk", "wv", "wo", "w1", "w2", "w3", "router_w",
              "ew1", "ew2", "ew3"):
        W["L1_ffn_" + k] = inputs["L1_ffn_" + k]
    for k in ("attn_norm_w", "ff_norm_w", "moe_norm_w"):
        W[k + "_bcast"] = np.ascontiguousarray(
            np.broadcast_to(inputs["L1_ffn_" + k][None, :], (128, D)))
    W["ew_all"] = np.ascontiguousarray(np.stack(
        [inputs["L1_ffn_ew1"], inputs["L1_ffn_ew3"], inputs["L1_ffn_ew2"]],
        axis=1))
    W["c_ident"] = np.eye(128, dtype=np.float32).astype(BF)
    dh = 64
    inv_freq = 1.0 / (10000.0 ** (np.arange(0, dh, 2, dtype=np.float32) / dh))
    t_pos = np.arange(256, dtype=np.float32)
    fr = t_pos[:, None] * inv_freq[None, :]
    fr = np.concatenate([fr, fr], axis=-1)
    cosf = np.cos(fr).astype(np.float32)
    sinf = np.sin(fr).astype(np.float32)
    t_glob = np.arange(4096, dtype=np.float32)
    fg = t_glob[:, None] * inv_freq[None, :]
    fg = np.concatenate([fg, fg], axis=-1)
    W["rope_gc"] = np.ascontiguousarray(np.cos(fg).T).astype(BF)
    W["rope_gs"] = np.ascontiguousarray(np.sin(fg).T).astype(BF)
    k_ = np.arange(128)[:, None]
    i_ = np.arange(128)[None, :]
    mcur = np.where(k_ <= i_, 0.0, -1e30).astype(np.float32)
    mbak = np.where(k_ >= i_, 0.0, -1e30).astype(np.float32)
    W["mask_l0T"] = np.ascontiguousarray(np.concatenate([mcur, mbak], axis=1))
    i_ = np.arange(128)[:, None]
    j_ = np.arange(256)[None, :]
    allowed = (j_ >= i_) & (j_ <= i_ + 128)
    mg = np.where(allowed, 0.0, -1e30).astype(np.float32)
    m0 = mg.copy(); m0[:, 0:128] = -1e30
    W["mask_g"] = mg
    W["mask_0"] = m0
    tri = np.where(j_[:, :128] <= i_, 0.0, -1e30).astype(np.float32)
    W["mask_tri"] = np.ascontiguousarray(tri)
    W["mask_triT"] = np.ascontiguousarray(tri.T)
    k_ = np.arange(128)[:, None]
    q_ = np.arange(512)[None, :]
    mqb = np.stack([np.where(j * 128 + k_ <= q_, 0.0, -1e30).astype(np.float32)
                    for j in range(4)], axis=0)
    W["mask_qb"] = np.ascontiguousarray(mqb)
    return W


def declare_weights(nc, W):
    aps = {}
    for k, v in W.items():
        dt = BF16 if v.dtype == BF else F32
        aps[k] = nc.dram_tensor(k, list(v.shape), dt, kind="ExternalInput").ap()
    return aps


def make_consts(ctx, aps):
    consts = {}
    ident = ctx.sbuf("ident", [128, 128], BF16)
    eps_col = ctx.sbuf("epsc", [128, 1], F32)
    t_id = ctx.Q.op(lambda q: q.dma_start(ident[:, :], aps["c_ident"]), inc=True)
    ctx.V.op(lambda v: v.memset(eps_col[:, :], 1e-5))
    epsb = ctx.sbuf("epsb", [1, 1], F32)
    ctx.V.op(lambda v: v.memset(epsb[:, :], 1e-30))
    consts["ident"] = ident
    consts["eps_col"] = eps_col
    consts["epsb"] = epsb
    consts["t_ident"] = t_id
    return consts



def build_l0_ffn_program(debug=False):
    nc = bass.Bass("TRN2", target_bir_lowering=False)
    stack = contextlib.ExitStack()
    x_in = nc.dram_tensor("x_in", [S, D], F32, kind="ExternalInput").ap()
    x_out = nc.dram_tensor("x_out", [S, D], F32, kind="ExternalOutput").ap()
    dbg = None
    if debug:
        dbg = {"gT": nc.dram_tensor("dbg_gT", [16, S], BF16, kind="ExternalOutput").ap(),
               "uT": nc.dram_tensor("dbg_uT", [128, NCH * S], BF16,
                                    kind="ExternalOutput").ap(),
               "rs": nc.dram_tensor("dbg_rs", [S, D], F32, kind="ExternalOutput").ap()}
    Wnames = ["L0_ffn_WrT", "L0_ffn_WsT", "L0_ffn_centT", "L0_ffn_br",
              "ebias_bcast", "bs0_bcast", "bs1_bcast", "c_ident"]
    aps = {}
    shapes = {"L0_ffn_WrT": [16, 1024, 1024], "L0_ffn_WsT": [2, 1024, 1024],
              "L0_ffn_centT": [1024, 16], "L0_ffn_br": [16, 1024],
              "ebias_bcast": [128, 16], "bs0_bcast": [128, 1024],
              "bs1_bcast": [128, 1024]}
    for k in Wnames:
        if k == "c_ident":
            aps[k] = nc.dram_tensor(k, [128, 128], BF16, kind="ExternalInput").ap()
        else:
            aps[k] = nc.dram_tensor(k, shapes[k], F32, kind="ExternalInput").ap()
    with stack:
        ctx = Ctx(nc, stack)
        consts = make_consts(ctx, aps)
        scr = emit_weightprep(ctx, aps, {"WrT", "WsumT"})
        emit_l0_ffn(ctx, x_in, x_out, aps, scr, consts, dbg=dbg)
        ctx.flush("fin")
    return nc


def build_l1_attn_program():
    nc = bass.Bass("TRN2", target_bir_lowering=False)
    stack = contextlib.ExitStack()
    x_in = nc.dram_tensor("x_in", [S, D], F32, kind="ExternalInput").ap()
    x_out = nc.dram_tensor("x_out", [S, D], F32, kind="ExternalOutput").ap()
    aps = {}
    for k in ("W_iT", "W_fT", "W_gT", "W_oT"):
        aps["L1_attn_" + k] = nc.dram_tensor("L1_attn_" + k, [1024, 1024], F32,
                                             kind="ExternalInput").ap()
    aps["c_ident"] = nc.dram_tensor("c_ident", [128, 128], BF16,
                                    kind="ExternalInput").ap()
    with stack:
        ctx = Ctx(nc, stack)
        consts = make_consts(ctx, aps)
        ctx.flush("init")
        emit_l1_attn(ctx, x_in, x_out, aps, consts)
        ctx.flush("fin")
    return nc


_L0FFN_W = ["L0_ffn_WrT", "L0_ffn_WsT", "L0_ffn_centT", "L0_ffn_br",
            "ebias_bcast", "bs0_bcast", "bs1_bcast"]
_L0ATTN_W = ["L0_attn_W_qkv", "L0_attn_W_out", "rope_gc", "rope_gs", "mask_l0T"]
_L1ATTN_W = ["L1_attn_W_iT", "L1_attn_W_fT", "L1_attn_W_gT", "L1_attn_W_oT"]
_L1FFN_W = (["L1_ffn_" + k for k in ("wq", "wk", "wv", "wo", "w1", "w2", "w3",
                                     "router_w")]
            + ["ew_all", "attn_norm_w_bcast", "ff_norm_w_bcast",
               "moe_norm_w_bcast", "mask_tri", "mask_triT", "mask_qb"])


def _declare(nc, W, names):
    aps = {}
    for k in names:
        v = W[k]
        dt = BF16 if v.dtype == BF else F32
        aps[k] = nc.dram_tensor(k, list(v.shape), dt, kind="ExternalInput").ap()
    aps["c_ident"] = nc.dram_tensor("c_ident", [128, 128], BF16,
                                    kind="ExternalInput").ap()
    return aps


def build_megakernel(W):
    nc = bass.Bass("TRN2", target_bir_lowering=False)
    stack = contextlib.ExitStack()
    x_in = nc.dram_tensor("x_in", [S, D], F32, kind="ExternalInput").ap()
    x_out = nc.dram_tensor("x_out", [S, D], F32, kind="ExternalOutput").ap()
    names = _L0FFN_W + _L0ATTN_W + _L1ATTN_W + _L1FFN_W
    aps = _declare(nc, W, names)
    with stack:
        ctx = Ctx(nc, stack)
        consts = make_consts(ctx, aps)
        scr = emit_weightprep(ctx, aps, {"WrT", "WsumT", "ewall", "w123"})
        xs = [ctx.dram(f"xc{i}", [S, D], F32) for i in range(5)]
        emit_l0_ffn(ctx, x_in, xs[0], aps, scr, consts, bg=scr["bg"])
        emit_l0_attn(ctx, xs[0], xs[1], aps, consts)
        emit_l0_ffn(ctx, xs[1], xs[2], aps, scr, consts)
        emit_l1_ffn(ctx, xs[2], xs[3], aps, scr, consts)
        emit_l1_attn(ctx, xs[3], xs[4], aps, consts)
        emit_l1_ffn(ctx, xs[4], x_out, aps, scr, consts)
        ctx.flush("fin")
    return nc


def _run_chain(W, x, programs):
    cur = [np.ascontiguousarray(x[b]) for b in range(8)]
    for build, names in programs:
        nc = build()
        in_maps = []
        for b in range(8):
            m = {"x_in": cur[b], "c_ident": W["c_ident"]}
            for k in names:
                m[k] = W[k]
            in_maps.append(m)
        res = run_bass_kernel_spmd(nc, in_maps, core_ids=list(range(8)))
        cur = [res.results[b]["x_out"] for b in range(8)]
    return np.stack(cur, axis=0)


FUSED = True


def kernel(**inputs):
    W = stage_inputs(inputs)
    x = np.ascontiguousarray(inputs["L0_attn_x"].astype(np.float32, copy=False))
    if not FUSED:
        programs = [
            (build_l0_ffn_program, _L0FFN_W),
            (build_l0_attn_program, _L0ATTN_W),
            (build_l0_ffn_program, _L0FFN_W),
            (build_l1_ffn_program, _L1FFN_W),
            (build_l1_attn_program, _L1ATTN_W),
            (build_l1_ffn_program, _L1FFN_W),
        ]
        return _run_chain(W, x, programs)
    nc = build_megakernel(W)
    names = _L0FFN_W + _L0ATTN_W + _L1ATTN_W + _L1FFN_W
    in_maps = []
    for b in range(8):
        m = {"x_in": np.ascontiguousarray(x[b]), "c_ident": W["c_ident"]}
        for k in names:
            m[k] = W[k]
        in_maps.append(m)
    res = run_bass_kernel_spmd(nc, in_maps, core_ids=list(range(8)))
    return np.stack([res.results[b]["x_out"] for b in range(8)], axis=0)
